# version 18
# baseline (speedup 1.0000x reference)
import numpy as np
import concourse.bass as bass
import concourse.mybir as mybir
from concourse.bass_utils import run_bass_kernel_spmd

F32 = mybir.dt.float32
BF16 = mybir.dt.bfloat16
AF = mybir.ActivationFunctionType
ALU = mybir.AluOpType
AX = mybir.AxisListType.X

P = 128
S = 2048
E = 2048
FF = 8192
nS = S // P
nE = E // P
nF = FF // P
W_ELEMS = E * FF


C_HI = float(np.float32(0.7))
C_LO = float(np.float32(np.float64(0.7) - np.float64(np.float32(0.7))))
_g = np.float32(C_HI) * np.float32(4097.0)
_d = np.float32(_g) - np.float32(C_HI)
CH1 = float(np.float32(_g) - np.float32(_d))
CH2 = float(np.float32(C_HI) - np.float32(CH1))
CSPLIT = 12582912.0
INV24 = float(2.0 ** -24)

N_QT = 256
ST_CUT = 232
QT_P2 = 32
QT_P3 = 64
QT_W1 = 128


class Ctr:

    def __init__(self, stack, nc, name):
        self.h = stack.enter_context(nc.semaphore(name))
        self.v = 0

    def inc(self, n=1):
        self.v += n
        return self.v


class DStream:

    def __init__(self, stack, nc, name, nsem=2):
        self.sems = [Ctr(stack, nc, "%s%d" % (name, i)) for i in range(nsem)]
        self.n = 0

    def issue(self, eng, out, in_):
        i = self.n
        self.n += 1
        s = self.sems[i % len(self.sems)]
        eng.dma_start(out=out, in_=in_).then_inc(s.h, 16)
        s.inc(16)
        return i

    def issue_idx(self, eng, out, in_, i):
        s = self.sems[i % len(self.sems)]
        eng.dma_start(out=out, in_=in_).then_inc(s.h, 16)
        s.inc(16)
        return i

    def wait_item(self, eng, i):
        k = len(self.sems)
        eng.wait_ge(self.sems[i % k].h, 16 * (i // k + 1))

    def wait_first_n(self, eng, n):
        k = len(self.sems)
        for j, s in enumerate(self.sems):
            cnt = (n - j + k - 1) // k
            if cnt > 0:
                eng.wait_ge(s.h, 16 * cnt)

    def wait_all(self, eng):
        for s in self.sems:
            if s.v > 0:
                eng.wait_ge(s.h, s.v)


def build_nc(upto=6, debug=False):
    import contextlib

    nc = bass.Bass("TRN2", target_bir_lowering=False)
    scratch_kind = "ExternalOutput" if debug else "Internal"

    x_tm = nc.dram_tensor("x_tm", [S, E], F32, kind="ExternalInput")
    x_fm = nc.dram_tensor("x_fm", [E, S], F32, kind="ExternalInput")
    wq = nc.dram_tensor("wq", [E, E], F32, kind="ExternalInput")
    wk = nc.dram_tensor("wk", [E, E], F32, kind="ExternalInput")
    wv = nc.dram_tensor("wv", [E, E], F32, kind="ExternalInput")
    wo = nc.dram_tensor("wo", [E, E], F32, kind="ExternalInput")
    w1t = nc.dram_tensor("w1t", [E, FF], F32, kind="ExternalInput")
    b1 = nc.dram_tensor("b1", [FF], F32, kind="ExternalInput")
    w2t = nc.dram_tensor("w2t", [FF, E], F32, kind="ExternalInput")
    b2 = nc.dram_tensor("b2", [E], F32, kind="ExternalInput")
    ident_in = nc.dram_tensor("ident_in", [P, P], F32, kind="ExternalInput")

    y = nc.dram_tensor("y", [S, E], F32, kind="ExternalOutput")

    q_h = nc.dram_tensor("q_h", [S, E], BF16, kind=scratch_kind)
    v_h = nc.dram_tensor("v_h", [E, S], BF16, kind=scratch_kind)
    x2_h = nc.dram_tensor("x2_h", [S, E], F32, kind=scratch_kind)
    h_h = nc.dram_tensor("h_h", [FF, S], BF16, kind=scratch_kind)
    w1q_h = nc.dram_tensor("w1q_h", [E, FF], BF16, kind=scratch_kind)
    w2q_h = nc.dram_tensor("w2q_h", [FF, E], BF16, kind=scratch_kind)

    def hap(t, off, dims):
        return bass.AP(t, off, dims)

    with contextlib.ExitStack() as stx:
        sem = lambda name: Ctr(stx, nc, name)
        strm = lambda name, k=2: DStream(stx, nc, name, k)

        g_xfm = strm("g_xfm"); g_id = strm("g_id", 1)
        g_wq = strm("g_wq"); g_wk = strm("g_wk")
        g_wo = strm("g_wo"); g_wo2 = strm("g_wo2")
        s_stats = strm("s_st", 5); s_qsl = strm("s_qsl"); s_vre = strm("s_vre", 4)
        s_osl = strm("s_osl"); s_x = strm("s_x"); s_x2i = strm("s_x2i", 8)
        s_w1sl = strm("s_w1sl"); s_w2g = strm("s_w2g"); s_hg = strm("s_hg")
        s_qin = strm("s_qin"); s_misc = strm("s_msc", 3); s_wv = strm("s_wv", 1)
        c_q = strm("c_q"); c_v = strm("c_v"); c_o = strm("c_o")
        c_x2 = strm("c_x2"); c_h = strm("c_h"); c_wq = strm("c_wq")
        c_y = strm("c_y")
        t_q = sem("t_q"); t_s = sem("t_s"); t_tr = sem("t_tr"); t_o = sem("t_o")
        t_y = sem("t_y"); t_tr4 = sem("t_tr4"); t_h = sem("t_h"); t_f = sem("t_f")
        t_fin = sem("t_fin")
        v_ev = sem("v_ev"); v_vv = sem("v_vv"); v_wacc = sem("v_wacc")
        v_rm = sem("v_rm"); v_rc = sem("v_rc"); v_qt = sem("v_qt")
        v_z = sem("v_z"); v_x2 = sem("v_x2"); v_fa = sem("v_fa"); v_ot = sem("v_ot")
        a_st = sem("a_st"); a_exp = sem("a_exp"); a_tcp = sem("a_tcp")
        a_oev = sem("a_oev"); a_trs = sem("a_trs"); a_x2b = sem("a_x2b")
        a_tfx = sem("a_tfx"); a_sq = sem("a_sq"); a_hh = sem("a_hh")
        a_sq2 = sem("a_sq2"); a_b1 = sem("a_b1")
        g_bc = sem("g_bc"); g_qm = sem("g_qm"); a_wv = sem("a_wv")
        a_ln = sem("a_ln"); v_rp = sem("v_rp"); a_ln2 = sem("a_ln2"); v_rp2 = sem("v_rp2")

        ident_bf = stx.enter_context(nc.sbuf_tensor("ident_bf", [P, P], BF16))
        ones_f = stx.enter_context(nc.sbuf_tensor("ones_f", [P, 1], F32))
        ones_bf = stx.enter_context(nc.sbuf_tensor("ones_bf", [1, P], BF16))
        ones_f2 = stx.enter_context(nc.sbuf_tensor("ones_f2", [1, P], F32))
        b1tmpb = stx.enter_context(nc.sbuf_tensor("b1tmpb", [nF, P], BF16))
        stats_acc = stx.enter_context(nc.sbuf_tensor("stats_acc", [P, 256], F32))
        wacc = stx.enter_context(nc.sbuf_tensor("wacc", [P, 4], F32))
        tvals = stx.enter_context(nc.sbuf_tensor("tvals", [1, 4], F32))
        tbc = stx.enter_context(nc.sbuf_tensor("tbc", [P, 8], F32))
        tq = stx.enter_context(nc.sbuf_tensor("tq", [1, 8], F32))
        thw = stx.enter_context(nc.sbuf_tensor("thw", [1, 12], F32))
        recip_all = stx.enter_context(nc.sbuf_tensor("recip_all", [P, nE], F32))
        nmax8 = stx.enter_context(nc.sbuf_tensor("nmax8", [P, 2], F32))
        rsum = stx.enter_context(nc.sbuf_tensor("rsum", [P, 2], F32))
        b1sb = stx.enter_context(nc.sbuf_tensor("b1sb", [P, nF], F32))
        b1tmp = stx.enter_context(nc.sbuf_tensor("b1tmp", [nF, P], F32))
        mv1 = stx.enter_context(nc.sbuf_tensor("mv1", [P, 2], F32))
        st6 = stx.enter_context(nc.sbuf_tensor("st6", [P, 24], F32))
        ve_t = stx.enter_context(nc.sbuf_tensor("ve_t", [P, 1], F32))
        rec_t = stx.enter_context(nc.sbuf_tensor("rec_t", [P, 1], F32))
        rstd_t = stx.enter_context(nc.sbuf_tensor("rstd_t", [P, 1], F32))
        mr_t = stx.enter_context(nc.sbuf_tensor("mr_t", [P, 1], F32))

        def w_tile_src(i):
            if i < 128:
                d_k, jj = i // 8, i % 8
                return hap(w1t, d_k * P * FF + jj * 1024, [[FF, P], [1, 1024]])
            j = i - 128
            f_k, jj = j // 2, j % 2
            return hap(w2t, f_k * P * E + jj * 1024, [[E, P], [1, 1024]])

        def wq_tile_dst(i):
            if i < 128:
                d_k, jj = i // 8, i % 8
                return hap(w1q_h, d_k * P * FF + jj * 1024, [[FF, P], [1, 1024]])
            j = i - 128
            f_k, jj = j // 2, j % 2
            return hap(w2q_h, f_k * P * E + jj * 1024, [[E, P], [1, 1024]])

        with nc.sbuf_tensor("big0", [P, nE * S], BF16) as big0, \
             nc.sbuf_tensor("qin", [P, 2 * 1024], F32) as qin, \
             nc.sbuf_tensor("qmask", [P, 1024], F32) as qmask, \
             nc.sbuf_tensor("qmask2", [P, 1024], F32) as qmask2, \
             nc.sbuf_tensor("wbuf", [P, 16384], BF16) as wbuf:

            qs_spilled = [0]

            def w_pair_src(p):
                i = 2 * p
                if i < 128:
                    d_k, jj = i // 8, i % 8
                    return hap(w1t, d_k * P * FF + jj * 1024, [[FF, P], [1, 2048]])
                j = i - 128
                f_k = j // 2
                return hap(w2t, f_k * P * E, [[E, P], [1, 2048]])

            def wq_pair_dst(p):
                i = 2 * p
                if i < 128:
                    d_k, jj = i // 8, i % 8
                    return hap(w1q_h, d_k * P * FF + jj * 1024, [[FF, P], [1, 2048]])
                j = i - 128
                f_k = j // 2
                return hap(w2q_h, f_k * P * E, [[E, P], [1, 2048]])

            def emit_quant_spills(eng, upto_tile):
                while (qs_spilled[0] + 1) * 2 <= upto_tile:
                    p = qs_spilled[0]
                    eng.wait_ge(v_qt.h, 2 * p + 2)
                    c_wq.issue(eng, wq_pair_dst(p),
                               qout[:, (p % 2) * 2048:(p % 2) * 2048 + 2048])
                    qs_spilled[0] += 1

            def emit_quant_loads(g, lo, hi):
                for p in range(lo // 2, hi // 2):
                    if p >= 2:
                        g.wait_ge(g_qm.h, 2 * (p - 1))
                    dst = qin if p % 2 == 0 else qin2
                    s_qin.issue(g, dst[:], w_pair_src(p))

            q_pend = []

            def emit_quant_tt(v):
                i, mka, need = q_pend.pop(0)
                qo = qout[:, (i % 4) * 1024:(i % 4) * 1024 + 1024]
                v.wait_ge(g_qm.h, need)
                v.tensor_tensor(qo, mka[:], qo, op=ALU.subtract).then_inc(v_qt.h, 1)
                v_qt.inc(1)

            def emit_quant_flush(v):
                while q_pend:
                    emit_quant_tt(v)

            def emit_quant_vec(v, lo, hi):
                for i in range(lo, hi):
                    j = 0 if i < 128 else 1
                    s_qin.wait_item(v, i // 2)
                    if i >= 2:
                        v.wait_ge(v_qt.h, i - 1)
                    qbuf = qin if (i // 2) % 2 == 0 else qin2
                    src = qbuf[:, (i % 2) * 1024:(i % 2) * 1024 + 1024]
                    mka = qmask if i % 2 == 0 else qmask2
                    v.tensor_scalar(mka[:], src, tbc[:, 4 * j:4 * j + 1],
                                    tbc[:, 4 * j + 1:4 * j + 2],
                                    op0=ALU.subtract, op1=ALU.is_ge)
                    if i >= 4:
                        c_wq.wait_item(v, i // 2 - 2)
                    v.tensor_scalar(qout[:, (i % 4) * 1024:(i % 4) * 1024 + 1024],
                                    src, tbc[:, 4 * j + 2:4 * j + 3],
                                    tbc[:, 4 * j + 3:4 * j + 4],
                                    op0=ALU.subtract, op1=ALU.is_le
                                    ).then_inc(g_qm.h, 1)
                    g_qm.inc(1)
                    q_pend.append((i, mka, g_qm.v))
                    if len(q_pend) > 1:
                        emit_quant_tt(v)


            with nc.sbuf_tensor("big1", [P, nE * S], BF16) as big1:

                with nc.psum_tensor("psA", [P, 2048], F32) as psA, \
                     nc.psum_tensor("psB", [P, 2048], F32) as psB, \
                     nc.sbuf_tensor("spillb", [P, 4096], BF16) as spillb, \
                     nc.sbuf_tensor("sbin", [P, 5 * 1024], F32) as sbin, \
                     nc.Block("p1", no_gpsimd_drain=True) as blk:

                    def qk_view(r):
                        t = psA if r < 4 else psB
                        return t[:, (r % 4) * 512:(r % 4) * 512 + 512]

                    @blk.gpsimd
                    def _(g):
                        g_id.issue(g, ident_bf[:], hap(ident_in, 0, [[P, P], [1, P]]))
                        for j in range(4):
                            g_xfm.issue(
                                g, big0[:, j * 4 * S:(j + 1) * 4 * S],
                                hap(x_fm, j * 4 * P * S, [[S, P], [P * S, 4], [1, S]]))
                        for i in range(8):
                            w = wq if i < 4 else wk
                            st_w = g_wq if i < 4 else g_wk
                            nb = i % 4
                            if i >= 2:
                                g.wait_ge(t_q.h, 16 * (i - 1))
                            st_w.issue(
                                g, wbuf[:, (i % 2) * 8192:(i % 2) * 8192 + 8192],
                                hap(w, nb * 512, [[E, P], [P * E, nE], [1, 512]]))
                        g.wait_ge(t_q.h, 144)
                        for j in range(2):
                            g_wo.issue(
                                g, wbuf[:, j * 8192:(j + 1) * 8192],
                                hap(wo, j * 4 * P * E, [[E, P], [P * E, 4], [1, E]]))

                    @blk.sync
                    def _(sy):
                        for i in (list(range(0, ST_CUT, 2))
                                  + list(range(ST_CUT, N_QT))):
                            if i >= 5:
                                sy.wait_ge(a_st.h, i - 4)
                            s_stats.issue_idx(
                                sy, sbin[:, (i % 5) * 1024:(i % 5) * 1024 + 1024],
                                w_tile_src(i), i)
                        s_misc.issue(sy, b1tmp[:], hap(b1, 0, [[P, nF], [1, P]]))

                    @blk.tensor
                    def _(t):
                        g_xfm.wait_all(t)
                        for i in range(8):
                            st_w = g_wq if i < 4 else g_wk
                            st_w.wait_item(t, i % 4)
                            half = wbuf[:, (i % 2) * 8192:(i % 2) * 8192 + 8192]
                            for s_m in range(nS):
                                ti = i * nS + s_m
                                if ti >= 8:
                                    t.wait_ge(v_ev.h, ti - 7)
                                pv = qk_view(ti % 8)
                                for e_k in range(nE):
                                    mm = t.matmul(
                                        pv,
                                        big0[:, e_k * S + s_m * P:e_k * S + s_m * P + P],
                                        half[:, e_k * 512:e_k * 512 + 512],
                                        start=(e_k == 0), stop=(e_k == nE - 1))
                                    if e_k == nE - 1:
                                        mm.then_inc(t_q.h, 1); t_q.inc(1)
                        t.wait_ge(v_ev.h, 128)
                        for f_m in range(nE):
                            t.wait_ge(a_wv.h, f_m + 1)
                            if f_m >= 2:
                                t.wait_ge(v_vv.h, f_m - 1)
                            ps = psA if f_m % 2 == 0 else psB
                            for e_k in range(nE):
                                for sn in range(4):
                                    mm = t.matmul(
                                        ps[:, sn * 512:sn * 512 + 512],
                                        wbuf[:, (f_m % 8) * 2048 + e_k * P:
                                             (f_m % 8) * 2048 + e_k * P + P],
                                        big0[:, e_k * S + sn * 512:
                                             e_k * S + sn * 512 + 512],
                                        start=(e_k == 0), stop=(e_k == nE - 1))
                                    if e_k == nE - 1 and sn == 3:
                                        mm.then_inc(t_q.h, 1); t_q.inc(1)
                        t.wait_ge(v_vv.h, 15)
                        t.wait_ge(v_wacc.h, 1)
                        t.matmul(psA[0:1, 0:4], ones_f[:, 0:1], wacc[:, 0:4],
                                 start=True, stop=True).then_inc(t_fin.h, 1)
                        t_fin.inc(1)

                    @blk.vector
                    def _(v):
                        v.memset(ones_f[:], 1.0)
                        v.memset(ones_bf[:], 1.0)
                        v.memset(ones_f2[:], 1.0)
                        v.memset(st6[:, 5:6], 1e-5)
                        for ti in range(128):
                            v.wait_ge(t_q.h, ti + 1)
                            i, s_m = ti // nS, ti % nS
                            pv = qk_view(ti % 8)
                            if i < 4:
                                qtr = s_m // 4
                                hidx = i * 4 + qtr
                                slot = hidx % 2
                                if hidx >= 2:
                                    c_q.wait_item(v, hidx - 2)
                                v.tensor_copy(
                                    spillb[:, slot * 2048 + (s_m % 4) * 512:
                                           slot * 2048 + (s_m % 4) * 512 + 512],
                                    pv).then_inc(v_ev.h, 1)
                            else:
                                nb = i - 4
                                v.tensor_copy(
                                    big1[:, s_m * S + nb * 512:s_m * S + nb * 512 + 512],
                                    pv).then_inc(v_ev.h, 1)
                            v_ev.inc(1)
                        for f_m in range(nE):
                            v.wait_ge(t_q.h, 128 + f_m + 1)
                            ps = psA if f_m % 2 == 0 else psB
                            if f_m < 2:
                                c_q.wait_first_n(v, 16)
                            if f_m >= 2:
                                c_v.wait_item(v, f_m - 2)
                            v.tensor_copy(spillb[:, (f_m % 2) * 2048:
                                                 (f_m % 2) * 2048 + 2048],
                                          ps[:]).then_inc(v_vv.h, 1)
                            v_vv.inc(1)
                        v.wait_ge(a_st.h, N_QT)
                        thr_hi = sbin[:, 0:128]
                        thr_lo = sbin[:, 1024:1152]
                        for j in range(2):
                            blkc = stats_acc[:, 128 * j:128 * j + 128]
                            v.tensor_scalar(thr_hi[:], blkc, CSPLIT, CSPLIT,
                                            op0=ALU.add, op1=ALU.subtract)
                            v.drain()
                            v.tensor_tensor(thr_lo[:], blkc, thr_hi[:],
                                            op=ALU.subtract)
                            v.drain()
                            v.reduce_sum(wacc[:, 2 * j:2 * j + 1], thr_hi[:], axis=AX)
                            mm_last = v.reduce_sum(wacc[:, 2 * j + 1:2 * j + 2],
                                                   thr_lo[:], axis=AX)
                            v.drain()
                            if j == 1:
                                mm_last.then_inc(v_wacc.h, 1)
                                v_wacc.inc(1)

                    @blk.scalar
                    def _(s):
                        def q_spill(hidx):
                            i, qtr = hidx // 4, hidx % 4
                            slot = hidx % 2
                            s.wait_ge(v_ev.h, i * nS + qtr * 4 + 4)
                            c_q.issue(
                                s,
                                hap(q_h, (qtr * 4 * P) * E + i * 512,
                                    [[E, P], [P * E, 4], [1, 512]]),
                                bass.AP(spillb, slot * 2048,
                                        [[4096, P], [512, 4], [1, 512]]))

                        spill_at = {}
                        for hidx in range(16):
                            thr = min(2 * (16 * (hidx // 4) + (hidx % 4) * 4 + 4),
                                      N_QT - 1)
                            spill_at.setdefault(thr, []).append(hidx)
                        s_wv.issue(s, qin[:, 0:2048],
                                   hap(wv, 0, [[E, P], [P * E, nE], [1, P]]))
                        s_stats.issue_idx(s, sbin[:, 1024:2048], w_tile_src(1), 1)
                        s_stats.issue_idx(s, sbin[:, 3072:4096], w_tile_src(3), 3)

                        def v_cast(f_m):
                            s_wv.wait_item(s, f_m)
                            s.wait_ge(t_q.h, 112 if f_m < 4 else
                                      (128 if f_m < 8 else 128 + (f_m - 8) + 1))
                            s.copy(wbuf[:, (f_m % 8) * 2048:(f_m % 8) * 2048 + 2048],
                                   qin[:, 0:2048]).then_inc(a_wv.h, 1)
                            a_wv.inc(1)
                            s.wait_ge(a_wv.h, a_wv.v)
                            if f_m < nE - 1:
                                s_wv.issue(s, qin[:, 0:2048],
                                           hap(wv, (f_m + 1) * P,
                                               [[E, P], [P * E, nE], [1, P]]))
                            if f_m >= 1:
                                s.wait_ge(v_vv.h, f_m)
                                c_v.issue(s, hap(v_h, (f_m - 1) * P * S,
                                                 [[S, P], [1, S]]),
                                          spillb[:, ((f_m - 1) % 2) * 2048:
                                                 ((f_m - 1) % 2) * 2048 + 2048])

                        def emit_stat(i):
                            s_stats.wait_item(s, i)
                            slot = sbin[:, (i % 5) * 1024:(i % 5) * 1024 + 1024]
                            s.activation(slot, slot,
                                         AF.Abs, accum_out=stats_acc[:, i:i + 1]
                                         ).then_inc(a_st.h, 1)
                            a_st.inc(1)
                            if i % 2 == 1 and i + 4 < ST_CUT:
                                s.wait_ge(a_st.h, i)
                                s_stats.issue_idx(s, sbin[:, ((i + 4) % 5) * 1024:
                                                          ((i + 4) % 5) * 1024 + 1024],
                                                  w_tile_src(i + 4), i + 4)
                            for hidx in spill_at.get(i, []):
                                q_spill(hidx)

                        for i in range(ST_CUT - 8):
                            emit_stat(i)
                        for f_m in range(4):
                            v_cast(f_m)
                        for i in range(ST_CUT - 8, ST_CUT):
                            emit_stat(i)
                        for f_m in range(4, 8):
                            v_cast(f_m)
                        k_st = ST_CUT
                        for f_m in range(8, nE):
                            v_cast(f_m)
                            for _ in range(3):
                                if k_st < N_QT:
                                    emit_stat(k_st)
                                    k_st += 1
                        while k_st < N_QT:
                            emit_stat(k_st)
                            k_st += 1
                        s.wait_ge(v_vv.h, nE)
                        c_v.issue(s, hap(v_h, (nE - 1) * P * S, [[S, P], [1, S]]),
                                  spillb[:, ((nE - 1) % 2) * 2048:
                                         ((nE - 1) % 2) * 2048 + 2048])
                        s.wait_ge(t_fin.h, 1)
                        s.copy(tvals[:], psA[0:1, 0:4]).then_inc(a_tcp.h, 1)
                        a_tcp.inc(1)

                with nc.sbuf_tensor("qout", [P, 4 * 1024], BF16) as qout, \
                     nc.sbuf_tensor("qin2a", [P, 2 * 1024], F32) as qin2:
                    if upto >= 2:
                        with nc.psum_tensor("psS0", [P, 2048], F32) as psS0, \
                             nc.psum_tensor("psS1", [P, 2048], F32) as psS1, \
                             nc.sbuf_tensor("qsl", [P, 2 * 2048], BF16) as qsl, \
                             nc.Block("p2", no_gpsimd_drain=True) as blk:

                            @blk.gpsimd
                            def _(g):
                                g.wait_ge(a_wv.h, nE)
                                g.wait_ge(g_bc.h, 1)
                                c_q.wait_all(g)
                                c_v.wait_all(g)
                                emit_quant_loads(g, 0, QT_P2)

                            @blk.sync
                            def _(sy):
                                c_q.wait_all(sy)
                                c_v.wait_first_n(sy, nE)
                                for e in range(nE):
                                    if e >= 2:
                                        sy.wait_ge(t_s.h, e - 1)
                                    s_qsl.issue(
                                        sy, qsl[:, (e % 2) * 2048:(e % 2) * 2048 + 2048],
                                        hap(q_h, e * P, [[E, P], [P * E, nS], [1, P]]))
                                    emit_quant_spills(sy, max(0, ((e - 1) * QT_P2) // nE))
                                emit_quant_spills(sy, QT_P2)
                                c_v.wait_all(sy)
                                sy.wait_ge(t_s.h, 16)
                                for j in (0, 2):
                                    s_vre.issue_idx(
                                        sy, big1[:, j * 4 * S:(j + 1) * 4 * S],
                                        hap(v_h, j * 4 * P * S,
                                            [[S, P], [P * S, 4], [1, S]]), j)

                            @blk.tensor
                            def _(t):
                                t.wait_ge(v_wacc.h, 2)
                                t.matmul(psS1[0:P, 0:8], ones_f2[0:1, 0:P],
                                         tq[0:1, 0:8], start=True, stop=True
                                         ).then_inc(g_bc.h, 1)
                                g_bc.inc(1)
                                for e in range(nE):
                                    s_qsl.wait_item(t, e)
                                    if e >= 2:
                                        t.wait_ge(a_exp.h, e - 1)
                                    if e == 1:
                                        t.wait_ge(v_wacc.h, 3)
                                    ps = psS0 if e % 2 == 0 else psS1
                                    for s_k in range(nS):
                                        for nb in range(4):
                                            mm = t.matmul(
                                                ps[:, nb * 512:nb * 512 + 512],
                                                qsl[:, (e % 2) * 2048 + s_k * P:
                                                    (e % 2) * 2048 + s_k * P + P],
                                                big1[:, s_k * S + nb * 512:
                                                     s_k * S + nb * 512 + 512],
                                                start=(s_k == 0), stop=(s_k == nS - 1))
                                            if s_k == nS - 1 and nb == 3:
                                                mm.then_inc(t_s.h, 1); t_s.inc(1)

                            @blk.vector
                            def _(v):
                                c_q.wait_all(v)
                                c_v.wait_all(v)
                                v.wait_ge(a_tcp.h, 1)

                                def ts1(dst, a, s, op):
                                    v.tensor_scalar(dst, a, s, None, op0=op)
                                    v.drain()

                                def tt1(dst, a, b, op):
                                    v.tensor_tensor(dst, a, b, op=op)
                                    v.drain()

                                W = lambda i: thw[:, i:i + 1]
                                for j in range(2):
                                    mh, ml, ga, de, mh1, mh2, T1, r, p, tmp =                                     W(0), W(1), W(2), W(3), W(4), W(5), W(6), W(7), W(8), W(9)
                                    ts1(mh, tvals[:, 2 * j:2 * j + 1], INV24, ALU.mult)
                                    ts1(ml, tvals[:, 2 * j + 1:2 * j + 2], INV24, ALU.mult)
                                    ts1(ga, mh, 4097.0, ALU.mult)
                                    tt1(de, ga, mh, ALU.subtract)
                                    tt1(mh1, ga, de, ALU.subtract)
                                    tt1(mh2, mh, mh1, ALU.subtract)
                                    ts1(T1, mh, C_HI, ALU.mult)
                                    ts1(r, mh1, CH1, ALU.mult)
                                    tt1(r, r, T1, ALU.subtract)
                                    ts1(p, mh2, CH1, ALU.mult)
                                    tt1(r, r, p, ALU.add)
                                    ts1(p, mh1, CH2, ALU.mult)
                                    tt1(r, r, p, ALU.add)
                                    ts1(p, mh2, CH2, ALU.mult)
                                    tt1(r, r, p, ALU.add)
                                    ts1(p, ml, C_HI, ALU.mult)
                                    tt1(r, r, p, ALU.add)
                                    ts1(p, mh, C_LO, ALU.mult)
                                    tt1(r, r, p, ALU.add)
                                    th, tl = W(10), W(11)
                                    tt1(th, T1, r, ALU.add)
                                    tt1(p, T1, th, ALU.subtract)
                                    tt1(tl, p, r, ALU.add)
                                    v.tensor_copy(tq[:, 4 * j:4 * j + 1], th)
                                    v.tensor_copy(tq[:, 4 * j + 1:4 * j + 2], tl)
                                    v.drain()
                                    ts1(tq[:, 4 * j + 2:4 * j + 3], th, -1.0, ALU.mult)
                                    ts1(tq[:, 4 * j + 3:4 * j + 4], tl, -1.0, ALU.mult)
                                v.nop()
                                v.tensor_copy(thw[:, 0:1], tq[:, 0:1]).then_inc(v_wacc.h, 1)
                                v_wacc.inc(1)
                                v.wait_ge(g_bc.h, 1)
                                v.tensor_copy(tbc[:], psS1[0:P, 0:8]).then_inc(v_wacc.h, 1)
                                v_wacc.inc(1)
                                v.drain()
                                for e in range(nE):
                                    v.wait_ge(t_s.h, e + 1)
                                    if e >= 2:
                                        v.wait_ge(a_exp.h, e - 1)
                                    ps = psS0 if e % 2 == 0 else psS1
                                    v.reduce_max(nmax8[:, e % 2:e % 2 + 1], ps[:], axis=AX)
                                    v.drain()
                                    v.tensor_scalar(nmax8[:, e % 2:e % 2 + 1],
                                                    nmax8[:, e % 2:e % 2 + 1],
                                                    -0.125, None, op0=ALU.mult
                                                    ).then_inc(v_rm.h, 1)
                                    v_rm.inc(1)
                                    if e >= 1:
                                        v.wait_ge(a_exp.h, e)
                                        v.reciprocal(recip_all[:, e - 1:e],
                                                     rsum[:, (e - 1) % 2:(e - 1) % 2 + 1]
                                                     ).then_inc(v_rc.h, 1)
                                        v_rc.inc(1)
                                    emit_quant_vec(v, (e * QT_P2) // nE,
                                                   ((e + 1) * QT_P2) // nE)
                                emit_quant_flush(v)
                                v.wait_ge(a_exp.h, nE)
                                v.reciprocal(recip_all[:, nE - 1:nE],
                                             rsum[:, (nE - 1) % 2:(nE - 1) % 2 + 1]
                                             ).then_inc(v_rc.h, 1)
                                v_rc.inc(1)

                            @blk.scalar
                            def _(s):
                                for e in range(nE):
                                    s.wait_ge(v_rm.h, e + 1)
                                    if e >= 2:
                                        s.wait_ge(v_rc.h, e - 1)
                                    ps = psS0 if e % 2 == 0 else psS1
                                    s.activation(big0[:, e * S:e * S + S], ps[:], AF.Exp,
                                                 bias=nmax8[:, e % 2:e % 2 + 1],
                                                 scale=0.125,
                                                 accum_out=rsum[:, e % 2:e % 2 + 1]
                                                 ).then_inc(a_exp.h, 1)
                                    a_exp.inc(1)
                                    if e == nE - 1:
                                        c_v.wait_all(s)
                                        s.wait_ge(t_s.h, 16)
                                        for j in (1, 3):
                                            s_vre.issue_idx(
                                                s, big1[:, j * 4 * S:(j + 1) * 4 * S],
                                                hap(v_h, j * 4 * P * S,
                                                    [[S, P], [P * S, 4], [1, S]]), j)

                    if upto >= 3:
                        with nc.psum_tensor("psO", [P, 2048], F32) as psO, \
                             nc.psum_tensor("psTA", [P, 2048], BF16) as psTA, \
                             nc.psum_tensor("psTB", [P, 2048], BF16) as psTB, \
                             nc.sbuf_tensor("atsl", [P, 2 * 2048], BF16) as atsl, \
                             nc.Block("p3", no_gpsimd_drain=True) as blk:

                            @blk.gpsimd
                            def _(g):
                                emit_quant_loads(g, QT_P2, QT_P2 + QT_P3)

                            @blk.sync
                            def _(sy):
                                for e_m in range(nE):
                                    emit_quant_spills(sy, QT_P2 + (e_m * QT_P3) // nE)
                                emit_quant_spills(sy, QT_P2 + QT_P3)

                            @blk.tensor
                            def _(t):
                                g_id.wait_all(t)

                                def trsp(e_m):
                                    pt = psTA if e_m % 2 == 0 else psTB
                                    if e_m >= 2:
                                        t.wait_ge(a_trs.h, e_m - 1)
                                    for f_k in range(nE):
                                        mm = t.transpose(pt[:, f_k * P:f_k * P + P],
                                                         big0[:, e_m * S + f_k * P:
                                                              e_m * S + f_k * P + P],
                                                         ident_bf[:])
                                        if f_k == nE - 1:
                                            mm.then_inc(t_tr.h, 1); t_tr.inc(1)

                                trsp(0)
                                trsp(1)
                                for e_m in range(nE):
                                    asl = atsl[:, (e_m % 2) * 2048:
                                               (e_m % 2) * 2048 + 2048]
                                    t.wait_ge(a_trs.h, e_m + 1)
                                    if e_m >= 1:
                                        t.wait_ge(a_oev.h, e_m)
                                    for f_k in range(nE):
                                        if e_m == 0 and f_k % 4 == 0:
                                            s_vre.wait_first_n(t, f_k // 4 + 1)
                                        for sn in range(4):
                                            mm = t.matmul(
                                                psO[:, sn * 512:sn * 512 + 512],
                                                asl[:, f_k * P:f_k * P + P],
                                                big1[:, f_k * S + sn * 512:
                                                     f_k * S + sn * 512 + 512],
                                                start=(f_k == 0), stop=(f_k == nE - 1))
                                            if f_k == nE - 1 and sn == 3:
                                                mm.then_inc(t_o.h, 1); t_o.inc(1)
                                    if e_m + 2 < nE:
                                        trsp(e_m + 2)

                            @blk.vector
                            def _(v):
                                for e_m in range(nE):
                                    emit_quant_vec(v, QT_P2 + (e_m * QT_P3) // nE,
                                                   QT_P2 + ((e_m + 1) * QT_P3) // nE)
                                emit_quant_flush(v)

                            @blk.scalar
                            def _(s):
                                s.wait_ge(v_rc.h, 16)
                                for e_m in range(nE):
                                    if e_m == 0:
                                        s.wait_ge(t_tr.h, 1)
                                        s.copy(atsl[:, 0:2048],
                                               psTA[:]).then_inc(a_trs.h, 1)
                                        a_trs.inc(1)
                                    if e_m + 1 < nE:
                                        s.wait_ge(t_tr.h, e_m + 2)
                                        if e_m >= 1:
                                            s.wait_ge(t_o.h, e_m)
                                        pt = psTA if (e_m + 1) % 2 == 0 else psTB
                                        s.copy(atsl[:, ((e_m + 1) % 2) * 2048:
                                                     ((e_m + 1) % 2) * 2048 + 2048],
                                               pt[:]).then_inc(a_trs.h, 1)
                                        a_trs.inc(1)
                                    s.wait_ge(t_o.h, e_m + 1)
                                    s.activation(big0[:, e_m * S:e_m * S + S],
                                                 psO[:], AF.Copy,
                                                 scale=recip_all[:, e_m:e_m + 1]
                                                 ).then_inc(a_oev.h, 1)
                                    a_oev.inc(1)

            with nc.sbuf_tensor("qout2", [P, 4 * 1024], BF16) as qout, \
                 nc.sbuf_tensor("qin2b", [P, 2 * 1024], F32) as qin2:
                if upto >= 4:
                    QT4_LO = QT_P2 + QT_P3
                    with contextlib.ExitStack() as st4:
                        ec4 = st4.enter_context
                        psy0 = ec4(nc.psum_tensor("psy0", [P, 512], F32))
                        psy1 = ec4(nc.psum_tensor("psy1", [P, 512], F32))
                        psy2 = ec4(nc.psum_tensor("psy2", [P, 512], F32))
                        psy3 = ec4(nc.psum_tensor("psy3", [P, 512], F32))
                        psy4 = ec4(nc.psum_tensor("psy4", [P, 512], F32))
                        psy5 = ec4(nc.psum_tensor("psy5", [P, 512], F32))
                        psT4 = ec4(nc.psum_tensor("psT4", [P, 2048], BF16))
                        xres = ec4(nc.sbuf_tensor("xres", [P, 2 * 2048], F32))
                        wobuf2 = ec4(nc.sbuf_tensor("wobuf2", [P, 16384], BF16))
                        x2t = ec4(nc.sbuf_tensor("x2t", [P, 2 * 2048], F32))
                        x2b = ec4(nc.sbuf_tensor("x2b", [P, 2 * 2048], BF16))
                        blk = ec4(nc.Block("p4", no_gpsimd_drain=True))

                        @blk.gpsimd
                        def _(g):
                            for j in range(2):
                                g_wo2.issue(
                                    g, wobuf2[:, j * 8192:(j + 1) * 8192],
                                    hap(wo, (8 + j * 4) * P * E,
                                        [[E, P], [P * E, 4], [1, E]]))
                            for s_m in range(nS):
                                if s_m >= 2:
                                    g.wait_ge(a_ln.h, s_m - 1)
                                s_x.issue(
                                    g, xres[:, (s_m % 2) * 2048:(s_m % 2) * 2048 + 2048],
                                    hap(x_tm, s_m * P * E, [[E, P], [1, E]]))
                                emit_quant_loads(
                                    g, QT4_LO + 2 * ((s_m * (QT_W1 - QT4_LO)) // nS // 2),
                                    QT4_LO + 2 * (((s_m + 1) * (QT_W1 - QT4_LO)) // nS // 2))

                        @blk.sync
                        def _(sy):
                            for s_m in range(nS):
                                emit_quant_spills(
                                    sy, QT4_LO + (s_m * (QT_W1 - QT4_LO)) // nS)
                            emit_quant_spills(sy, QT_W1)

                        @blk.tensor
                        def _(t):
                            g_wo.wait_all(t)
                            g_id.wait_all(t)

                            def x2_transposes(pm):
                                t.wait_ge(a_x2b.h, pm + 1)
                                if pm >= 1:
                                    t.wait_ge(a_tfx.h, pm)
                                xb = x2b[:, (pm % 2) * 2048:(pm % 2) * 2048 + 2048]
                                for d_k in range(nE):
                                    mm = t.transpose(psT4[:, d_k * P:d_k * P + P],
                                                     xb[:, d_k * P:d_k * P + P],
                                                     ident_bf[:])
                                    if d_k == nE - 1:
                                        mm.then_inc(t_tr4.h, 1); t_tr4.inc(1)

                            psy = [psy0, psy1, psy2, psy3, psy4, psy5]
                            t.wait_ge(a_oev.h, nE)
                            for s_m in range(nS):
                                if s_m >= 1:
                                    t.wait_ge(v_z.h, max(0, 4 * s_m - 2))
                                for e_k in range(nE):
                                    if s_m == 0 and e_k == 8:
                                        g_wo2.wait_all(t)
                                    wsrc = wbuf if e_k < 8 else wobuf2
                                    wof = (e_k % 8) * E
                                    for nb in range(4):
                                        mm = t.matmul(
                                            psy[(4 * s_m + nb) % 6][:],
                                            big0[:, e_k * S + s_m * P:e_k * S + s_m * P + P],
                                            wsrc[:, wof + nb * 512:wof + nb * 512 + 512],
                                            start=(e_k == 0), stop=(e_k == nE - 1))
                                        if e_k == nE - 1 and nb == 3:
                                            mm.then_inc(t_y.h, 1); t_y.inc(1)
                                if s_m >= 2:
                                    x2_transposes(s_m - 2)
                            x2_transposes(nS - 2)
                            x2_transposes(nS - 1)
                            t.wait_ge(a_b1.h, 1)
                            t.wait_ge(a_tfx.h, 16)
                            t.transpose(psT4[:, 0:nF], b1tmpb[:], ident_bf[0:nF, 0:nF]
                                        ).then_inc(t_fin.h, 1)
                            t_fin.inc(1)

                        @blk.vector
                        def _(v):
                            c_wq.wait_first_n(v, (QT_P2 + QT_P3) // 2)
                            psy = [psy0, psy1, psy2, psy3, psy4, psy5]
                            for s_m in range(nS):
                                slot = x2t[:, (s_m % 2) * 2048:(s_m % 2) * 2048 + 2048]
                                lo = QT4_LO + (s_m * (QT_W1 - QT4_LO)) // nS
                                hi = QT4_LO + ((s_m + 1) * (QT_W1 - QT4_LO)) // nS
                                mid = (lo + hi) // 2
                                v.wait_ge(t_y.h, s_m + 1)
                                s_x.wait_item(v, s_m)
                                if s_m >= 2:
                                    c_x2.wait_item(v, s_m - 2)
                                    v.wait_ge(a_x2b.h, s_m - 1)
                                    v.wait_ge(a_ln.h, s_m - 1)
                                for nb in range(4):
                                    v.scalar_tensor_tensor(
                                        slot[:, nb * 512:nb * 512 + 512],
                                        psy[(4 * s_m + nb) % 6][:], 0.0,
                                        xres[:, (s_m % 2) * 2048 + nb * 512:
                                             (s_m % 2) * 2048 + nb * 512 + 512],
                                        op0=ALU.add, op1=ALU.add,
                                        accum_out=stats_acc[:, 4 * (s_m % 2) + nb:
                                                            4 * (s_m % 2) + nb + 1]
                                        ).then_inc(v_z.h, 1)
                                    v_z.inc(1)
                                emit_quant_vec(v, lo, mid)
                                v.wait_ge(a_ln.h, s_m + 1)
                                v.reciprocal(rec_t[:], ve_t[:]).then_inc(v_rp.h, 1)
                                v_rp.inc(1)
                                emit_quant_vec(v, mid, hi)
                                v.wait_ge(a_sq.h, s_m + 1)
                                v.tensor_scalar(slot, slot, rstd_t[:, 0:1], mr_t[:, 0:1],
                                                op0=ALU.mult, op1=ALU.add
                                                ).then_inc(v_x2.h, 1)
                                v_x2.inc(1)
                            emit_quant_flush(v)

                        @blk.scalar
                        def _(s):
                            s_misc.wait_item(s, 0)
                            s.copy(b1tmpb[:], b1tmp[:]).then_inc(a_b1.h, 1)
                            a_b1.inc(1)

                            def x2fm_evacs(pm):
                                s.wait_ge(t_tr4.h, pm + 1)
                                s.wait_ge(t_y.h, pm + 1)
                                s.copy(bass.AP(big0, pm * P, [[nE * S, P], [S, nE], [1, P]]),
                                       psT4[:]).then_inc(a_tfx.h, 1)
                                a_tfx.inc(1)

                            for s_m in range(nS):
                                sink = xres[:, (s_m % 2) * 2048:(s_m % 2) * 2048 + 2048]
                                slot = x2t[:, (s_m % 2) * 2048:(s_m % 2) * 2048 + 2048]
                                s.wait_ge(v_z.h, 4 * (s_m + 1))
                                if s_m >= 1:
                                    s.wait_ge(v_rp.h, s_m)
                                s.activation(sink, slot, AF.Square,
                                             accum_out=mv1[:, 1:2])
                                s.drain()
                                qb = 4 * (s_m % 2)
                                s.add(st6[:, 6:7], stats_acc[:, qb:qb + 1],
                                      stats_acc[:, qb + 1:qb + 2])
                                s.drain()
                                s.add(st6[:, 7:8], stats_acc[:, qb + 2:qb + 3],
                                      stats_acc[:, qb + 3:qb + 4])
                                s.drain()
                                s.add(mv1[:, 0:1], st6[:, 6:7], st6[:, 7:8])
                                s.drain()
                                s.mul(st6[:, 0:1], mv1[:, 0:1], 1.0 / 2048.0)
                                s.mul(st6[:, 1:2], mv1[:, 1:2], 1.0 / 2048.0)
                                s.drain()
                                s.square(st6[:, 2:3], st6[:, 0:1])
                                s.mul(st6[:, 4:5], st6[:, 0:1], -1.0)
                                s.drain()
                                s.activation(st6[:, 3:4], st6[:, 2:3], AF.Identity,
                                             scale=-1.0, bias=st6[:, 5:6])
                                s.drain()
                                s.add(ve_t[:], st6[:, 1:2], st6[:, 3:4]
                                      ).then_inc(a_ln.h, 1)
                                a_ln.inc(1)
                                s.wait_ge(v_rp.h, s_m + 1)
                                if s_m >= 1:
                                    s.wait_ge(v_x2.h, s_m)
                                s.activation(rstd_t[:], rec_t[:], AF.Sqrt)
                                s.drain()
                                s.mul(mr_t[:], st6[:, 4:5], rstd_t[:, 0:1]
                                      ).then_inc(a_sq.h, 1)
                                a_sq.inc(1)
                                s.wait_ge(v_x2.h, s_m + 1)
                                if s_m >= 2:
                                    s.wait_ge(t_tr4.h, s_m - 1)
                                s.copy(x2b[:, (s_m % 2) * 2048:(s_m % 2) * 2048 + 2048],
                                       x2t[:, (s_m % 2) * 2048:(s_m % 2) * 2048 + 2048]
                                       ).then_inc(a_x2b.h, 1)
                                a_x2b.inc(1)
                                c_x2.issue(s, hap(x2_h, s_m * P * E, [[E, P], [1, E]]),
                                           x2t[:, (s_m % 2) * 2048:(s_m % 2) * 2048 + 2048])
                                if s_m >= 2:
                                    x2fm_evacs(s_m - 2)
                            x2fm_evacs(nS - 2)
                            x2fm_evacs(nS - 1)
                            s.wait_ge(t_fin.h, 2)
                            s.copy(b1sb[:, 0:nF], psT4[:, 0:nF]).then_inc(a_b1.h, 1)
                            a_b1.inc(1)

                if upto >= 5:
                    with nc.psum_tensor("psH0", [P, 2048], F32) as psH0, \
                         nc.psum_tensor("psH1", [P, 2048], F32) as psH1, \
                         nc.sbuf_tensor("w1sl", [P, 2 * 2048], BF16) as w1sl, \
                         nc.sbuf_tensor("hstg", [P, 2 * 2048], BF16) as hstg, \
                         nc.Block("f1", no_gpsimd_drain=True) as blk:

                        @blk.gpsimd
                        def _(g):
                            emit_quant_loads(g, QT_W1, N_QT)

                        @blk.sync
                        def _(sy):
                            c_x2.wait_all(sy)
                            for f_m in range(nF):
                                if f_m == 0:
                                    c_wq.wait_first_n(sy, QT_W1 // 2)
                                if f_m >= 2:
                                    sy.wait_ge(t_h.h, f_m - 1)
                                s_w1sl.issue(
                                    sy, w1sl[:, (f_m % 2) * 2048:(f_m % 2) * 2048 + 2048],
                                    hap(w1q_h, f_m * P, [[FF, P], [P * FF, nE], [1, P]]))
                                if f_m >= 1:
                                    emit_quant_spills(sy, QT_W1 + 2 * (f_m - 1))
                            emit_quant_spills(sy, N_QT)

                        @blk.vector
                        def _(v):
                            for f_m in range(nF):
                                emit_quant_vec(v, QT_W1 + 2 * f_m, QT_W1 + 2 * f_m + 2)
                            emit_quant_flush(v)

                        @blk.tensor
                        def _(t):
                            for f_m in range(nF):
                                s_w1sl.wait_item(t, f_m)
                                if f_m >= 2:
                                    t.wait_ge(a_hh.h, f_m - 1)
                                ps = psH0 if f_m % 2 == 0 else psH1
                                for d_k in range(nE):
                                    for nb in range(4):
                                        mm = t.matmul(
                                            ps[:, nb * 512:nb * 512 + 512],
                                            w1sl[:, (f_m % 2) * 2048 + d_k * P:
                                                 (f_m % 2) * 2048 + d_k * P + P],
                                            big0[:, d_k * S + nb * 512:
                                                 d_k * S + nb * 512 + 512],
                                            start=(d_k == 0), stop=(d_k == nE - 1))
                                        if d_k == nE - 1 and nb == 3:
                                            mm.then_inc(t_h.h, 1); t_h.inc(1)

                        @blk.scalar
                        def _(s):
                            for f_m in range(nF):
                                s.wait_ge(t_h.h, f_m + 1)
                                if f_m >= 2:
                                    c_h.wait_item(s, f_m - 2)
                                ps = psH0 if f_m % 2 == 0 else psH1
                                s.activation(hstg[:, (f_m % 2) * 2048:
                                                  (f_m % 2) * 2048 + 2048],
                                             ps[:], AF.Gelu, bias=b1sb[:, f_m:f_m + 1]
                                             ).then_inc(a_hh.h, 1)
                                a_hh.inc(1)
                                s.wait_ge(a_hh.h, f_m + 1)
                                c_h.issue(s, hap(h_h, f_m * P * S, [[S, P], [1, S]]),
                                          hstg[:, (f_m % 2) * 2048:(f_m % 2) * 2048 + 2048])

        if upto >= 6:
            FG = 8
            nhalf = nS // 2
            with nc.psum_tensor("psF0", [P, 2048], F32) as psF0, \
                 nc.psum_tensor("psF1", [P, 2048], F32) as psF1, \
                 nc.sbuf_tensor("oacc", [P, 8 * 2048], F32) as oacc, \
                 nc.sbuf_tensor("w2g", [P, 2 * 8 * 2048], BF16) as w2g, \
                 nc.sbuf_tensor("hg", [P, 2 * 8 * 1024], BF16) as hg, \
                 nc.sbuf_tensor("lnsink", [P, 2048], F32) as lnsink, \
                 nc.sbuf_tensor("outt", [P, 2 * 2048], F32) as outt, \
                 nc.sbuf_tensor("b2bf", [1, E], BF16) as b2bf, \
                 nc.Block("f2", no_gpsimd_drain=True) as blk:

                @blk.gpsimd
                def _(g):
                    g_id.issue(g, b2bf[:], hap(b2, 0, [[E, 1], [1, E]]))

                @blk.sync
                def _(sy):
                    c_wq.wait_all(sy)
                    c_h.wait_all(sy)

                    def grp_loads(gi):
                        half, g = gi // FG, gi % FG
                        if gi >= 2:
                            sy.wait_ge(t_f.h, (gi - 1) * nhalf)
                        c_h.wait_first_n(sy, 8 * (g + 1))
                        slot = gi % 2
                        s_w2g.issue(
                            sy, w2g[:, slot * 16384:slot * 16384 + 16384],
                            hap(w2q_h, (g * 8) * P * E, [[E, P], [P * E, 8], [1, E]]))
                        if gi > 0:
                            s_hg.issue_idx(
                                sy, hg[:, slot * 8192:slot * 8192 + 8192],
                                hap(h_h, (g * 8) * P * S + half * 1024,
                                    [[S, P], [P * S, 8], [1, 1024]]), gi)

                    grp_loads(0)
                    c_x2.wait_all(sy)
                    for j in range(nhalf):
                        s_x2i.issue(sy, oacc[:, j * 2048:j * 2048 + 2048],
                                    hap(x2_h, j * P * E, [[E, P], [1, E]]))
                    for gi in range(1, FG):
                        grp_loads(gi)
                    grp_loads(FG)
                    for j in range(nhalf):
                        sy.wait_ge(v_ot.h, j + 1)
                        s_x2i.issue(sy, oacc[:, j * 2048:j * 2048 + 2048],
                                    hap(x2_h, (nhalf + j) * P * E, [[E, P], [1, E]]))
                    for gi in range(FG + 1, 2 * FG):
                        grp_loads(gi)

                @blk.tensor
                def _(t):
                    g_id.wait_all(t)
                    for half in range(2):
                        for g in range(FG):
                            gi = half * FG + g
                            slot = gi % 2
                            s_w2g.wait_item(t, gi)
                            s_hg.wait_item(t, gi)
                            for j in range(nhalf):
                                ti = gi * nhalf + j
                                if ti >= 2:
                                    t.wait_ge(v_fa.h, ti - 1)
                                ps = psF0 if ti % 2 == 0 else psF1
                                if g == 0:
                                    for nb in range(4):
                                        t.matmul(ps[:, nb * 512:nb * 512 + 512],
                                                 ones_bf[:, 0:P],
                                                 b2bf[:, nb * 512:nb * 512 + 512],
                                                 start=True, stop=False)
                                for f_k in range(8):
                                    for nb in range(4):
                                        mm = t.matmul(
                                            ps[:, nb * 512:nb * 512 + 512],
                                            hg[:, slot * 8192 + f_k * 1024 + j * P:
                                                slot * 8192 + f_k * 1024 + j * P + P],
                                            w2g[:, slot * 16384 + f_k * E + nb * 512:
                                                slot * 16384 + f_k * E + nb * 512 + 512],
                                            start=(g != 0 and f_k == 0),
                                            stop=(f_k == 7))
                                        if f_k == 7 and nb == 3:
                                            mm.then_inc(t_f.h, 1); t_f.inc(1)

                @blk.vector
                def _(v):
                    def ln2_recip(hj):
                        v.wait_ge(a_ln2.h, hj + 1)
                        if hj >= 1:
                            v.wait_ge(a_sq2.h, hj)
                        v.reciprocal(rec_t[:], ve_t[:]).then_inc(v_rp2.h, 1)
                        v_rp2.inc(1)

                    def ln2_affine(hj):
                        j = hj % nhalf
                        v.wait_ge(a_sq2.h, hj + 1)
                        if hj >= 2:
                            c_y.wait_item(v, hj - 2)
                        v.tensor_scalar(outt[:, (hj % 2) * 2048:
                                            (hj % 2) * 2048 + 2048],
                                        oacc[:, j * 2048:j * 2048 + 2048],
                                        rstd_t[:, 0:1], mr_t[:, 0:1],
                                        op0=ALU.mult, op1=ALU.add
                                        ).then_inc(v_ot.h, 1)
                        v_ot.inc(1)

                    for half in range(2):
                        for g in range(FG):
                            gi = half * FG + g
                            for j in range(nhalf):
                                ti = gi * nhalf + j
                                v.wait_ge(t_f.h, ti + 1)
                                if g == 0:
                                    s_x2i.wait_item(v, half * nhalf + j)
                                ps = psF0 if ti % 2 == 0 else psF1
                                if g == FG - 1:
                                    v.scalar_tensor_tensor(
                                        oacc[:, j * 2048:j * 2048 + 2048], ps[:],
                                        0.0, oacc[:, j * 2048:j * 2048 + 2048],
                                        op0=ALU.add, op1=ALU.add,
                                        accum_out=stats_acc[:, 8 + j:9 + j]
                                        ).then_inc(v_fa.h, 1)
                                else:
                                    v.tensor_tensor(
                                        oacc[:, j * 2048:j * 2048 + 2048], ps[:],
                                        oacc[:, j * 2048:j * 2048 + 2048],
                                        op=ALU.add).then_inc(v_fa.h, 1)
                                v_fa.inc(1)
                                if g == FG - 1:
                                    if j >= 4:
                                        ln2_affine(half * nhalf + j - 4)
                                    if j >= 2:
                                        ln2_recip(half * nhalf + j - 2)
                        for j in range(nhalf, nhalf + 4):
                            ln2_affine(half * nhalf + j - 4)
                            if j - 2 < nhalf:
                                ln2_recip(half * nhalf + j - 2)

                @blk.scalar
                def _(s):
                    c_h.wait_all(s)
                    s_hg.issue_idx(
                        s, hg[:, 0:8192],
                        hap(h_h, 0, [[S, P], [P * S, 8], [1, 1024]]), 0)
                    nm_col = [st6[:, 4 + 2 * (k % 2):5 + 2 * (k % 2)] for k in range(2)]

                    def ln2_stats(hj):
                        half, j = hj // nhalf, hj % nhalf
                        strip = oacc[:, j * 2048:j * 2048 + 2048]
                        last_ti = ((half + 1) * FG - 1) * nhalf + j
                        s.wait_ge(v_fa.h, last_ti + 1)
                        if hj >= 1:
                            s.wait_ge(v_rp2.h, hj)
                        s.activation(lnsink[:], strip, AF.Square,
                                     accum_out=mv1[:, 1:2])
                        s.drain()
                        s.mul(st6[:, 0:1], stats_acc[:, 8 + j:9 + j], 1.0 / 2048.0)
                        s.mul(st6[:, 1:2], mv1[:, 1:2], 1.0 / 2048.0)
                        s.drain()
                        s.square(st6[:, 2:3], st6[:, 0:1])
                        s.mul(nm_col[hj % 2][:], st6[:, 0:1], -1.0)
                        s.drain()
                        s.activation(st6[:, 3:4], st6[:, 2:3], AF.Identity,
                                     scale=-1.0, bias=st6[:, 5:6])
                        s.drain()
                        s.add(ve_t[:], st6[:, 1:2], st6[:, 3:4]
                              ).then_inc(a_ln2.h, 1)
                        a_ln2.inc(1)

                    def ln2_sqrt(hj):
                        s.wait_ge(v_rp2.h, hj + 1)
                        if hj >= 1:
                            s.wait_ge(v_ot.h, hj)
                        s.activation(rstd_t[:], rec_t[:], AF.Sqrt)
                        s.drain()
                        s.mul(mr_t[:], nm_col[hj % 2][:], rstd_t[:, 0:1]
                              ).then_inc(a_sq2.h, 1)
                        a_sq2.inc(1)

                    def ln2_spill(hj):
                        s.wait_ge(v_ot.h, hj + 1)
                        c_y.issue(s, hap(y, hj * P * E, [[E, P], [1, E]]),
                                  outt[:, (hj % 2) * 2048:(hj % 2) * 2048 + 2048])

                    for half in range(2):
                        for k in range(nhalf):
                            hj = half * nhalf + k
                            ln2_stats(hj)
                            if k >= 1:
                                ln2_sqrt(hj - 1)
                            if k >= 2:
                                ln2_spill(hj - 2)
                        ln2_sqrt(half * nhalf + nhalf - 1)
                        ln2_spill(half * nhalf + nhalf - 2)
                        ln2_spill(half * nhalf + nhalf - 1)
                    c_y.wait_all(s)

        with nc.Block("fin", no_gpsimd_drain=True) as blk:
            @blk.sync
            def _(sy):
                for st_ in (g_xfm, g_id, g_wq, g_wk, g_wo, g_wo2, s_wv,
                            s_stats, s_qsl, s_vre, s_osl, s_x, s_x2i, s_w1sl,
                            s_w2g, s_hg, s_qin, s_misc,
                            c_q, c_v, c_o, c_x2, c_h, c_wq, c_y):
                    st_.wait_all(sy)

    return nc


_NC_CACHE = {}


def _get_nc(upto=6, debug=False):
    key = (upto, debug)
    if key not in _NC_CACHE:
        _NC_CACHE[key] = build_nc(upto, debug)
    return _NC_CACHE[key]


def make_in_maps(x, Wq, Wk, Wv, Wo, w1T, b1v, w2T, b2v):
    ident = np.eye(P, dtype=np.float32)
    in_maps = []
    for c in range(x.shape[0]):
        xc = np.ascontiguousarray(x[c])
        in_maps.append({
            "x_tm": xc,
            "x_fm": np.ascontiguousarray(xc.T),
            "wq": Wq, "wk": Wk, "wv": Wv, "wo": Wo,
            "w1t": w1T, "b1": b1v, "w2t": w2T, "b2": b2v,
            "ident_in": ident,
        })
    return in_maps


def kernel(L0_attn_x, L0_attn_Wq, L0_attn_Wk, L0_attn_Wv, L0_attn_Wo,
           L0_ffn_w1, L0_ffn_b1, L0_ffn_w2, L0_ffn_b2):
    x = np.ascontiguousarray(np.asarray(L0_attn_x, dtype=np.float32))
    Wq = np.ascontiguousarray(np.asarray(L0_attn_Wq, dtype=np.float32))
    Wk = np.ascontiguousarray(np.asarray(L0_attn_Wk, dtype=np.float32))
    Wv = np.ascontiguousarray(np.asarray(L0_attn_Wv, dtype=np.float32))
    Wo = np.ascontiguousarray(np.asarray(L0_attn_Wo, dtype=np.float32))
    w1T = np.ascontiguousarray(np.asarray(L0_ffn_w1, dtype=np.float32).T)
    w2T = np.ascontiguousarray(np.asarray(L0_ffn_w2, dtype=np.float32).T)
    b1v = np.ascontiguousarray(np.asarray(L0_ffn_b1, dtype=np.float32))
    b2v = np.ascontiguousarray(np.asarray(L0_ffn_b2, dtype=np.float32))

    nc = _get_nc()
    in_maps = make_in_maps(x, Wq, Wk, Wv, Wo, w1T, b1v, w2T, b2v)
    res = run_bass_kernel_spmd(nc, in_maps, core_ids=list(range(8)))
    return np.stack([np.asarray(r["y"]) for r in res.results], axis=0)


# version 19
# speedup vs baseline: 1.0224x; 1.0224x over previous
import numpy as np
import concourse.bass as bass
import concourse.mybir as mybir
from concourse.bass_utils import run_bass_kernel_spmd

F32 = mybir.dt.float32
BF16 = mybir.dt.bfloat16
AF = mybir.ActivationFunctionType
ALU = mybir.AluOpType
AX = mybir.AxisListType.X

P = 128
S = 2048
E = 2048
FF = 8192
nS = S // P
nE = E // P
nF = FF // P
W_ELEMS = E * FF


C_HI = float(np.float32(0.7))
C_LO = float(np.float32(np.float64(0.7) - np.float64(np.float32(0.7))))
_g = np.float32(C_HI) * np.float32(4097.0)
_d = np.float32(_g) - np.float32(C_HI)
CH1 = float(np.float32(_g) - np.float32(_d))
CH2 = float(np.float32(C_HI) - np.float32(CH1))
CSPLIT = 12582912.0
INV24 = float(2.0 ** -24)

N_QT = 256
ST_CUT = 232
QT_P2 = 32
QT_P3 = 64
QT_W1 = 128


class Ctr:

    def __init__(self, stack, nc, name):
        self.h = stack.enter_context(nc.semaphore(name))
        self.v = 0

    def inc(self, n=1):
        self.v += n
        return self.v


class DStream:

    def __init__(self, stack, nc, name, nsem=2):
        self.sems = [Ctr(stack, nc, "%s%d" % (name, i)) for i in range(nsem)]
        self.n = 0

    def issue(self, eng, out, in_):
        i = self.n
        self.n += 1
        s = self.sems[i % len(self.sems)]
        eng.dma_start(out=out, in_=in_).then_inc(s.h, 16)
        s.inc(16)
        return i

    def issue_idx(self, eng, out, in_, i):
        s = self.sems[i % len(self.sems)]
        eng.dma_start(out=out, in_=in_).then_inc(s.h, 16)
        s.inc(16)
        return i

    def wait_item(self, eng, i):
        k = len(self.sems)
        eng.wait_ge(self.sems[i % k].h, 16 * (i // k + 1))

    def wait_first_n(self, eng, n):
        k = len(self.sems)
        for j, s in enumerate(self.sems):
            cnt = (n - j + k - 1) // k
            if cnt > 0:
                eng.wait_ge(s.h, 16 * cnt)

    def wait_all(self, eng):
        for s in self.sems:
            if s.v > 0:
                eng.wait_ge(s.h, s.v)


def build_nc(upto=6, debug=False):
    import contextlib

    nc = bass.Bass("TRN2", target_bir_lowering=False)
    scratch_kind = "ExternalOutput" if debug else "Internal"

    x_tm = nc.dram_tensor("x_tm", [S, E], F32, kind="ExternalInput")
    x_fm = nc.dram_tensor("x_fm", [E, S], F32, kind="ExternalInput")
    wq = nc.dram_tensor("wq", [E, E], F32, kind="ExternalInput")
    wk = nc.dram_tensor("wk", [E, E], F32, kind="ExternalInput")
    wv = nc.dram_tensor("wv", [E, E], F32, kind="ExternalInput")
    wo = nc.dram_tensor("wo", [E, E], F32, kind="ExternalInput")
    w1t = nc.dram_tensor("w1t", [E, FF], F32, kind="ExternalInput")
    b1 = nc.dram_tensor("b1", [FF], F32, kind="ExternalInput")
    w2t = nc.dram_tensor("w2t", [FF, E], F32, kind="ExternalInput")
    b2 = nc.dram_tensor("b2", [E], F32, kind="ExternalInput")
    ident_in = nc.dram_tensor("ident_in", [P, P], F32, kind="ExternalInput")

    y = nc.dram_tensor("y", [S, E], F32, kind="ExternalOutput")

    q_h = nc.dram_tensor("q_h", [S, E], BF16, kind=scratch_kind)
    v_h = nc.dram_tensor("v_h", [E, S], BF16, kind=scratch_kind)
    x2_h = nc.dram_tensor("x2_h", [S, E], F32, kind=scratch_kind)
    h_h = nc.dram_tensor("h_h", [FF, S], BF16, kind=scratch_kind)
    w1q_h = nc.dram_tensor("w1q_h", [E, FF], BF16, kind=scratch_kind)
    w2q_h = nc.dram_tensor("w2q_h", [FF, E], BF16, kind=scratch_kind)

    def hap(t, off, dims):
        return bass.AP(t, off, dims)

    with contextlib.ExitStack() as stx:
        sem = lambda name: Ctr(stx, nc, name)
        strm = lambda name, k=2: DStream(stx, nc, name, k)

        g_xfm = strm("g_xfm"); g_id = strm("g_id", 1)
        g_wq = strm("g_wq", 4); g_wk = strm("g_wk", 4)
        g_wo = strm("g_wo"); g_wo2 = strm("g_wo2")
        s_stats = strm("s_st", 5); s_qsl = strm("s_qsl"); s_vre = strm("s_vre", 4)
        s_osl = strm("s_osl"); s_x = strm("s_x"); s_x2i = strm("s_x2i", 8)
        s_w1sl = strm("s_w1sl"); s_w2g = strm("s_w2g"); s_hg = strm("s_hg")
        s_qin = strm("s_qin"); s_misc = strm("s_msc", 3); s_wv = strm("s_wv", 1)
        c_q = strm("c_q"); c_v = strm("c_v"); c_o = strm("c_o")
        c_x2 = strm("c_x2"); c_h = strm("c_h"); c_wq = strm("c_wq")
        c_y = strm("c_y")
        t_q = sem("t_q"); t_s = sem("t_s"); t_tr = sem("t_tr"); t_o = sem("t_o")
        t_y = sem("t_y"); t_tr4 = sem("t_tr4"); t_h = sem("t_h"); t_f = sem("t_f")
        t_fin = sem("t_fin")
        v_ev = sem("v_ev"); v_vv = sem("v_vv"); v_wacc = sem("v_wacc")
        v_rm = sem("v_rm"); v_rc = sem("v_rc"); v_qt = sem("v_qt")
        v_z = sem("v_z"); v_x2 = sem("v_x2"); v_fa = sem("v_fa"); v_ot = sem("v_ot")
        a_st = sem("a_st"); a_exp = sem("a_exp"); a_tcp = sem("a_tcp")
        a_oev = sem("a_oev"); a_trs = sem("a_trs"); a_x2b = sem("a_x2b")
        a_tfx = sem("a_tfx"); a_sq = sem("a_sq"); a_hh = sem("a_hh")
        a_sq2 = sem("a_sq2"); a_b1 = sem("a_b1")
        g_bc = sem("g_bc"); g_qm = sem("g_qm"); a_wv = sem("a_wv")
        a_ln = sem("a_ln"); v_rp = sem("v_rp"); a_ln2 = sem("a_ln2"); v_rp2 = sem("v_rp2")

        ident_bf = stx.enter_context(nc.sbuf_tensor("ident_bf", [P, P], BF16))
        ones_f = stx.enter_context(nc.sbuf_tensor("ones_f", [P, 1], F32))
        ones_bf = stx.enter_context(nc.sbuf_tensor("ones_bf", [1, P], BF16))
        ones_f2 = stx.enter_context(nc.sbuf_tensor("ones_f2", [1, P], F32))
        b1tmpb = stx.enter_context(nc.sbuf_tensor("b1tmpb", [nF, P], BF16))
        stats_acc = stx.enter_context(nc.sbuf_tensor("stats_acc", [P, 256], F32))
        wacc = stx.enter_context(nc.sbuf_tensor("wacc", [P, 4], F32))
        tvals = stx.enter_context(nc.sbuf_tensor("tvals", [1, 4], F32))
        tbc = stx.enter_context(nc.sbuf_tensor("tbc", [P, 8], F32))
        tq = stx.enter_context(nc.sbuf_tensor("tq", [1, 8], F32))
        thw = stx.enter_context(nc.sbuf_tensor("thw", [1, 12], F32))
        recip_all = stx.enter_context(nc.sbuf_tensor("recip_all", [P, nE], F32))
        nmax8 = stx.enter_context(nc.sbuf_tensor("nmax8", [P, 2], F32))
        rsum = stx.enter_context(nc.sbuf_tensor("rsum", [P, 2], F32))
        b1sb = stx.enter_context(nc.sbuf_tensor("b1sb", [P, nF], F32))
        b1tmp = stx.enter_context(nc.sbuf_tensor("b1tmp", [nF, P], F32))
        mv1 = stx.enter_context(nc.sbuf_tensor("mv1", [P, 2], F32))
        st6 = stx.enter_context(nc.sbuf_tensor("st6", [P, 24], F32))
        ve_t = stx.enter_context(nc.sbuf_tensor("ve_t", [P, 1], F32))
        rec_t = stx.enter_context(nc.sbuf_tensor("rec_t", [P, 1], F32))
        rstd_t = stx.enter_context(nc.sbuf_tensor("rstd_t", [P, 1], F32))
        mr_t = stx.enter_context(nc.sbuf_tensor("mr_t", [P, 1], F32))

        def w_tile_src(i):
            if i < 128:
                d_k, jj = i // 8, i % 8
                return hap(w1t, d_k * P * FF + jj * 1024, [[FF, P], [1, 1024]])
            j = i - 128
            f_k, jj = j // 2, j % 2
            return hap(w2t, f_k * P * E + jj * 1024, [[E, P], [1, 1024]])

        def wq_tile_dst(i):
            if i < 128:
                d_k, jj = i // 8, i % 8
                return hap(w1q_h, d_k * P * FF + jj * 1024, [[FF, P], [1, 1024]])
            j = i - 128
            f_k, jj = j // 2, j % 2
            return hap(w2q_h, f_k * P * E + jj * 1024, [[E, P], [1, 1024]])

        with nc.sbuf_tensor("big0", [P, nE * S], BF16) as big0, \
             nc.sbuf_tensor("qin", [P, 2 * 1024], F32) as qin, \
             nc.sbuf_tensor("qmask", [P, 1024], F32) as qmask, \
             nc.sbuf_tensor("qmask2", [P, 1024], F32) as qmask2, \
             nc.sbuf_tensor("wbuf", [P, 16384], BF16) as wbuf:

            qs_spilled = [0]

            def w_pair_src(p):
                i = 2 * p
                if i < 128:
                    d_k, jj = i // 8, i % 8
                    return hap(w1t, d_k * P * FF + jj * 1024, [[FF, P], [1, 2048]])
                j = i - 128
                f_k = j // 2
                return hap(w2t, f_k * P * E, [[E, P], [1, 2048]])

            def wq_pair_dst(p):
                i = 2 * p
                if i < 128:
                    d_k, jj = i // 8, i % 8
                    return hap(w1q_h, d_k * P * FF + jj * 1024, [[FF, P], [1, 2048]])
                j = i - 128
                f_k = j // 2
                return hap(w2q_h, f_k * P * E, [[E, P], [1, 2048]])

            def emit_quant_spills(eng, upto_tile):
                while (qs_spilled[0] + 1) * 2 <= upto_tile:
                    p = qs_spilled[0]
                    eng.wait_ge(v_qt.h, 2 * p + 2)
                    c_wq.issue(eng, wq_pair_dst(p),
                               qout[:, (p % 2) * 2048:(p % 2) * 2048 + 2048])
                    qs_spilled[0] += 1

            def emit_quant_loads(g, lo, hi):
                for p in range(lo // 2, hi // 2):
                    if p >= 2:
                        g.wait_ge(g_qm.h, 2 * (p - 1))
                    dst = qin if p % 2 == 0 else qin2
                    s_qin.issue(g, dst[:], w_pair_src(p))

            q_pend = []

            def emit_quant_tt(v):
                i, mka, need = q_pend.pop(0)
                qo = qout[:, (i % 4) * 1024:(i % 4) * 1024 + 1024]
                v.wait_ge(g_qm.h, need)
                v.tensor_tensor(qo, mka[:], qo, op=ALU.subtract).then_inc(v_qt.h, 1)
                v_qt.inc(1)

            def emit_quant_flush(v):
                while q_pend:
                    emit_quant_tt(v)

            def emit_quant_vec(v, lo, hi):
                for i in range(lo, hi):
                    j = 0 if i < 128 else 1
                    s_qin.wait_item(v, i // 2)
                    if i >= 2:
                        v.wait_ge(v_qt.h, i - 1)
                    qbuf = qin if (i // 2) % 2 == 0 else qin2
                    src = qbuf[:, (i % 2) * 1024:(i % 2) * 1024 + 1024]
                    mka = qmask if i % 2 == 0 else qmask2
                    v.tensor_scalar(mka[:], src, tbc[:, 4 * j:4 * j + 1],
                                    tbc[:, 4 * j + 1:4 * j + 2],
                                    op0=ALU.subtract, op1=ALU.is_ge)
                    if i >= 4:
                        c_wq.wait_item(v, i // 2 - 2)
                    v.tensor_scalar(qout[:, (i % 4) * 1024:(i % 4) * 1024 + 1024],
                                    src, tbc[:, 4 * j + 2:4 * j + 3],
                                    tbc[:, 4 * j + 3:4 * j + 4],
                                    op0=ALU.subtract, op1=ALU.is_le
                                    ).then_inc(g_qm.h, 1)
                    g_qm.inc(1)
                    q_pend.append((i, mka, g_qm.v))
                    if len(q_pend) > 1:
                        emit_quant_tt(v)


            with nc.sbuf_tensor("big1", [P, nE * S], BF16) as big1:

                with nc.psum_tensor("psA", [P, 2048], F32) as psA, \
                     nc.psum_tensor("psB", [P, 2048], F32) as psB, \
                     nc.sbuf_tensor("spillb", [P, 4096], BF16) as spillb, \
                     nc.sbuf_tensor("sbin", [P, 5 * 1024], F32) as sbin, \
                     nc.Block("p1", no_gpsimd_drain=True) as blk:

                    def qk_view(r):
                        t = psA if r < 4 else psB
                        return t[:, (r % 4) * 512:(r % 4) * 512 + 512]

                    @blk.gpsimd
                    def _(g):
                        g_id.issue(g, ident_bf[:], hap(ident_in, 0, [[P, P], [1, P]]))
                        for j in range(4):
                            g_xfm.issue(
                                g, big0[:, j * 4 * S:(j + 1) * 4 * S],
                                hap(x_fm, j * 4 * P * S, [[S, P], [P * S, 4], [1, S]]))
                        for i in range(8):
                            w = wq if i < 4 else wk
                            st_w = g_wq if i < 4 else g_wk
                            nb = i % 4
                            if i >= 2:
                                g.wait_ge(t_q.h, 16 * (i - 1))
                            for h in range(2):
                                st_w.issue_idx(
                                    g, wbuf[:, (i % 2) * 8192 + h * 4096:
                                            (i % 2) * 8192 + h * 4096 + 4096],
                                    hap(w, nb * 512 + h * 8 * P * E,
                                        [[E, P], [P * E, 8], [1, 512]]),
                                    2 * (i % 4) + h)
                        g.wait_ge(t_q.h, 144)
                        for j in range(2):
                            g_wo.issue(
                                g, wbuf[:, j * 8192:(j + 1) * 8192],
                                hap(wo, j * 4 * P * E, [[E, P], [P * E, 4], [1, E]]))

                    @blk.sync
                    def _(sy):
                        for i in (list(range(0, ST_CUT, 2))
                                  + list(range(ST_CUT, N_QT))):
                            if i >= 5:
                                sy.wait_ge(a_st.h, i - 4)
                            s_stats.issue_idx(
                                sy, sbin[:, (i % 5) * 1024:(i % 5) * 1024 + 1024],
                                w_tile_src(i), i)
                        s_misc.issue(sy, b1tmp[:], hap(b1, 0, [[P, nF], [1, P]]))

                    @blk.tensor
                    def _(t):
                        g_xfm.wait_all(t)
                        for i in range(8):
                            st_w = g_wq if i < 4 else g_wk
                            st_w.wait_item(t, 2 * (i % 4))
                            half = wbuf[:, (i % 2) * 8192:(i % 2) * 8192 + 8192]
                            for s_m in range(nS):
                                ti = i * nS + s_m
                                if ti >= 8:
                                    t.wait_ge(v_ev.h, ti - 7)
                                pv = qk_view(ti % 8)
                                for e_k in range(nE):
                                    if s_m == 0 and e_k == 8:
                                        st_w.wait_item(t, 2 * (i % 4) + 1)
                                    mm = t.matmul(
                                        pv,
                                        big0[:, e_k * S + s_m * P:e_k * S + s_m * P + P],
                                        half[:, e_k * 512:e_k * 512 + 512],
                                        start=(e_k == 0), stop=(e_k == nE - 1))
                                    if e_k == nE - 1:
                                        mm.then_inc(t_q.h, 1); t_q.inc(1)
                        t.wait_ge(v_ev.h, 128)
                        for f_m in range(nE):
                            t.wait_ge(a_wv.h, f_m + 1)
                            if f_m >= 2:
                                t.wait_ge(v_vv.h, f_m - 1)
                            ps = psA if f_m % 2 == 0 else psB
                            for e_k in range(nE):
                                for sn in range(4):
                                    mm = t.matmul(
                                        ps[:, sn * 512:sn * 512 + 512],
                                        wbuf[:, (f_m % 8) * 2048 + e_k * P:
                                             (f_m % 8) * 2048 + e_k * P + P],
                                        big0[:, e_k * S + sn * 512:
                                             e_k * S + sn * 512 + 512],
                                        start=(e_k == 0), stop=(e_k == nE - 1))
                                    if e_k == nE - 1 and sn == 3:
                                        mm.then_inc(t_q.h, 1); t_q.inc(1)
                        t.wait_ge(v_vv.h, 15)
                        t.wait_ge(v_wacc.h, 1)
                        t.matmul(psA[0:1, 0:4], ones_f[:, 0:1], wacc[:, 0:4],
                                 start=True, stop=True).then_inc(t_fin.h, 1)
                        t_fin.inc(1)

                    @blk.vector
                    def _(v):
                        v.memset(ones_f[:], 1.0)
                        v.memset(ones_bf[:], 1.0)
                        v.memset(ones_f2[:], 1.0)
                        v.memset(st6[:, 5:6], 1e-5)
                        for ti in range(128):
                            v.wait_ge(t_q.h, ti + 1)
                            i, s_m = ti // nS, ti % nS
                            pv = qk_view(ti % 8)
                            if i < 4:
                                qtr = s_m // 4
                                hidx = i * 4 + qtr
                                slot = hidx % 2
                                if hidx >= 2:
                                    c_q.wait_item(v, hidx - 2)
                                v.tensor_copy(
                                    spillb[:, slot * 2048 + (s_m % 4) * 512:
                                           slot * 2048 + (s_m % 4) * 512 + 512],
                                    pv).then_inc(v_ev.h, 1)
                            else:
                                nb = i - 4
                                v.tensor_copy(
                                    big1[:, s_m * S + nb * 512:s_m * S + nb * 512 + 512],
                                    pv).then_inc(v_ev.h, 1)
                            v_ev.inc(1)
                        for f_m in range(nE):
                            v.wait_ge(t_q.h, 128 + f_m + 1)
                            ps = psA if f_m % 2 == 0 else psB
                            if f_m < 2:
                                c_q.wait_first_n(v, 16)
                            if f_m >= 2:
                                c_v.wait_item(v, f_m - 2)
                            v.tensor_copy(spillb[:, (f_m % 2) * 2048:
                                                 (f_m % 2) * 2048 + 2048],
                                          ps[:]).then_inc(v_vv.h, 1)
                            v_vv.inc(1)
                        v.wait_ge(a_st.h, N_QT)
                        thr_hi = sbin[:, 0:128]
                        thr_lo = sbin[:, 1024:1152]
                        for j in range(2):
                            blkc = stats_acc[:, 128 * j:128 * j + 128]
                            v.tensor_scalar(thr_hi[:], blkc, CSPLIT, CSPLIT,
                                            op0=ALU.add, op1=ALU.subtract)
                            v.drain()
                            v.tensor_tensor(thr_lo[:], blkc, thr_hi[:],
                                            op=ALU.subtract)
                            v.drain()
                            v.reduce_sum(wacc[:, 2 * j:2 * j + 1], thr_hi[:], axis=AX)
                            mm_last = v.reduce_sum(wacc[:, 2 * j + 1:2 * j + 2],
                                                   thr_lo[:], axis=AX)
                            v.drain()
                            if j == 1:
                                mm_last.then_inc(v_wacc.h, 1)
                                v_wacc.inc(1)

                    @blk.scalar
                    def _(s):
                        def q_spill(hidx):
                            i, qtr = hidx // 4, hidx % 4
                            slot = hidx % 2
                            s.wait_ge(v_ev.h, i * nS + qtr * 4 + 4)
                            c_q.issue(
                                s,
                                hap(q_h, (qtr * 4 * P) * E + i * 512,
                                    [[E, P], [P * E, 4], [1, 512]]),
                                bass.AP(spillb, slot * 2048,
                                        [[4096, P], [512, 4], [1, 512]]))

                        spill_at = {}
                        for hidx in range(16):
                            thr = min(2 * (16 * (hidx // 4) + (hidx % 4) * 4 + 4),
                                      N_QT - 1)
                            spill_at.setdefault(thr, []).append(hidx)
                        s_wv.issue(s, qin[:, 0:2048],
                                   hap(wv, 0, [[E, P], [P * E, nE], [1, P]]))
                        s_stats.issue_idx(s, sbin[:, 1024:2048], w_tile_src(1), 1)
                        s_stats.issue_idx(s, sbin[:, 3072:4096], w_tile_src(3), 3)

                        def v_cast(f_m):
                            s_wv.wait_item(s, f_m)
                            s.wait_ge(t_q.h, 112 if f_m < 4 else
                                      (128 if f_m < 8 else 128 + (f_m - 8) + 1))
                            s.copy(wbuf[:, (f_m % 8) * 2048:(f_m % 8) * 2048 + 2048],
                                   qin[:, 0:2048]).then_inc(a_wv.h, 1)
                            a_wv.inc(1)
                            s.wait_ge(a_wv.h, a_wv.v)
                            if f_m < nE - 1:
                                s_wv.issue(s, qin[:, 0:2048],
                                           hap(wv, (f_m + 1) * P,
                                               [[E, P], [P * E, nE], [1, P]]))
                            if f_m >= 1:
                                s.wait_ge(v_vv.h, f_m)
                                c_v.issue(s, hap(v_h, (f_m - 1) * P * S,
                                                 [[S, P], [1, S]]),
                                          spillb[:, ((f_m - 1) % 2) * 2048:
                                                 ((f_m - 1) % 2) * 2048 + 2048])

                        def emit_stat(i):
                            s_stats.wait_item(s, i)
                            slot = sbin[:, (i % 5) * 1024:(i % 5) * 1024 + 1024]
                            s.activation(slot, slot,
                                         AF.Abs, accum_out=stats_acc[:, i:i + 1]
                                         ).then_inc(a_st.h, 1)
                            a_st.inc(1)
                            if i % 2 == 1 and i + 4 < ST_CUT:
                                s.wait_ge(a_st.h, i)
                                s_stats.issue_idx(s, sbin[:, ((i + 4) % 5) * 1024:
                                                          ((i + 4) % 5) * 1024 + 1024],
                                                  w_tile_src(i + 4), i + 4)
                            for hidx in spill_at.get(i, []):
                                q_spill(hidx)

                        for i in range(ST_CUT - 8):
                            emit_stat(i)
                        for f_m in range(4):
                            v_cast(f_m)
                        for i in range(ST_CUT - 8, ST_CUT):
                            emit_stat(i)
                        for f_m in range(4, 8):
                            v_cast(f_m)
                        k_st = ST_CUT
                        for f_m in range(8, nE):
                            v_cast(f_m)
                            for _ in range(3):
                                if k_st < N_QT:
                                    emit_stat(k_st)
                                    k_st += 1
                        while k_st < N_QT:
                            emit_stat(k_st)
                            k_st += 1
                        s.wait_ge(v_vv.h, nE)
                        c_v.issue(s, hap(v_h, (nE - 1) * P * S, [[S, P], [1, S]]),
                                  spillb[:, ((nE - 1) % 2) * 2048:
                                         ((nE - 1) % 2) * 2048 + 2048])
                        s.wait_ge(t_fin.h, 1)
                        s.copy(tvals[:], psA[0:1, 0:4]).then_inc(a_tcp.h, 1)
                        a_tcp.inc(1)

                with nc.sbuf_tensor("qout", [P, 4 * 1024], BF16) as qout, \
                     nc.sbuf_tensor("qin2a", [P, 2 * 1024], F32) as qin2:
                    if upto >= 2:
                        with nc.psum_tensor("psS0", [P, 2048], F32) as psS0, \
                             nc.psum_tensor("psS1", [P, 2048], F32) as psS1, \
                             nc.sbuf_tensor("qsl", [P, 2 * 2048], BF16) as qsl, \
                             nc.Block("p2", no_gpsimd_drain=True) as blk:

                            @blk.gpsimd
                            def _(g):
                                g.wait_ge(a_wv.h, nE)
                                g.wait_ge(g_bc.h, 1)
                                c_q.wait_all(g)
                                c_v.wait_all(g)
                                emit_quant_loads(g, 0, QT_P2)

                            @blk.sync
                            def _(sy):
                                c_q.wait_all(sy)
                                c_v.wait_first_n(sy, nE)
                                for e in range(nE):
                                    if e >= 2:
                                        sy.wait_ge(t_s.h, e - 1)
                                    s_qsl.issue(
                                        sy, qsl[:, (e % 2) * 2048:(e % 2) * 2048 + 2048],
                                        hap(q_h, e * P, [[E, P], [P * E, nS], [1, P]]))
                                    emit_quant_spills(sy, max(0, ((e - 1) * QT_P2) // nE))
                                emit_quant_spills(sy, QT_P2)
                                c_v.wait_all(sy)
                                sy.wait_ge(t_s.h, 16)
                                for j in (0, 2):
                                    s_vre.issue_idx(
                                        sy, big1[:, j * 4 * S:(j + 1) * 4 * S],
                                        hap(v_h, j * 4 * P * S,
                                            [[S, P], [P * S, 4], [1, S]]), j)

                            @blk.tensor
                            def _(t):
                                t.wait_ge(v_wacc.h, 2)
                                t.matmul(psS1[0:P, 0:8], ones_f2[0:1, 0:P],
                                         tq[0:1, 0:8], start=True, stop=True
                                         ).then_inc(g_bc.h, 1)
                                g_bc.inc(1)
                                for e in range(nE):
                                    s_qsl.wait_item(t, e)
                                    if e >= 2:
                                        t.wait_ge(a_exp.h, e - 1)
                                    if e == 1:
                                        t.wait_ge(v_wacc.h, 3)
                                    ps = psS0 if e % 2 == 0 else psS1
                                    for s_k in range(nS):
                                        for nb in range(4):
                                            mm = t.matmul(
                                                ps[:, nb * 512:nb * 512 + 512],
                                                qsl[:, (e % 2) * 2048 + s_k * P:
                                                    (e % 2) * 2048 + s_k * P + P],
                                                big1[:, s_k * S + nb * 512:
                                                     s_k * S + nb * 512 + 512],
                                                start=(s_k == 0), stop=(s_k == nS - 1))
                                            if s_k == nS - 1 and nb == 3:
                                                mm.then_inc(t_s.h, 1); t_s.inc(1)

                            @blk.vector
                            def _(v):
                                c_q.wait_all(v)
                                c_v.wait_all(v)
                                v.wait_ge(a_tcp.h, 1)

                                def ts1(dst, a, s, op):
                                    v.tensor_scalar(dst, a, s, None, op0=op)
                                    v.drain()

                                def tt1(dst, a, b, op):
                                    v.tensor_tensor(dst, a, b, op=op)
                                    v.drain()

                                W = lambda i: thw[:, i:i + 1]
                                for j in range(2):
                                    mh, ml, ga, de, mh1, mh2, T1, r, p, tmp =                                     W(0), W(1), W(2), W(3), W(4), W(5), W(6), W(7), W(8), W(9)
                                    ts1(mh, tvals[:, 2 * j:2 * j + 1], INV24, ALU.mult)
                                    ts1(ml, tvals[:, 2 * j + 1:2 * j + 2], INV24, ALU.mult)
                                    ts1(ga, mh, 4097.0, ALU.mult)
                                    tt1(de, ga, mh, ALU.subtract)
                                    tt1(mh1, ga, de, ALU.subtract)
                                    tt1(mh2, mh, mh1, ALU.subtract)
                                    ts1(T1, mh, C_HI, ALU.mult)
                                    ts1(r, mh1, CH1, ALU.mult)
                                    tt1(r, r, T1, ALU.subtract)
                                    ts1(p, mh2, CH1, ALU.mult)
                                    tt1(r, r, p, ALU.add)
                                    ts1(p, mh1, CH2, ALU.mult)
                                    tt1(r, r, p, ALU.add)
                                    ts1(p, mh2, CH2, ALU.mult)
                                    tt1(r, r, p, ALU.add)
                                    ts1(p, ml, C_HI, ALU.mult)
                                    tt1(r, r, p, ALU.add)
                                    ts1(p, mh, C_LO, ALU.mult)
                                    tt1(r, r, p, ALU.add)
                                    th, tl = W(10), W(11)
                                    tt1(th, T1, r, ALU.add)
                                    tt1(p, T1, th, ALU.subtract)
                                    tt1(tl, p, r, ALU.add)
                                    v.tensor_copy(tq[:, 4 * j:4 * j + 1], th)
                                    v.tensor_copy(tq[:, 4 * j + 1:4 * j + 2], tl)
                                    v.drain()
                                    ts1(tq[:, 4 * j + 2:4 * j + 3], th, -1.0, ALU.mult)
                                    ts1(tq[:, 4 * j + 3:4 * j + 4], tl, -1.0, ALU.mult)
                                v.nop()
                                v.tensor_copy(thw[:, 0:1], tq[:, 0:1]).then_inc(v_wacc.h, 1)
                                v_wacc.inc(1)
                                v.wait_ge(g_bc.h, 1)
                                v.tensor_copy(tbc[:], psS1[0:P, 0:8]).then_inc(v_wacc.h, 1)
                                v_wacc.inc(1)
                                v.drain()
                                for e in range(nE):
                                    v.wait_ge(t_s.h, e + 1)
                                    if e >= 2:
                                        v.wait_ge(a_exp.h, e - 1)
                                    ps = psS0 if e % 2 == 0 else psS1
                                    v.reduce_max(nmax8[:, e % 2:e % 2 + 1], ps[:], axis=AX)
                                    v.drain()
                                    v.tensor_scalar(nmax8[:, e % 2:e % 2 + 1],
                                                    nmax8[:, e % 2:e % 2 + 1],
                                                    -0.125, None, op0=ALU.mult
                                                    ).then_inc(v_rm.h, 1)
                                    v_rm.inc(1)
                                    if e >= 1:
                                        v.wait_ge(a_exp.h, e)
                                        v.reciprocal(recip_all[:, e - 1:e],
                                                     rsum[:, (e - 1) % 2:(e - 1) % 2 + 1]
                                                     ).then_inc(v_rc.h, 1)
                                        v_rc.inc(1)
                                    emit_quant_vec(v, (e * QT_P2) // nE,
                                                   ((e + 1) * QT_P2) // nE)
                                emit_quant_flush(v)
                                v.wait_ge(a_exp.h, nE)
                                v.reciprocal(recip_all[:, nE - 1:nE],
                                             rsum[:, (nE - 1) % 2:(nE - 1) % 2 + 1]
                                             ).then_inc(v_rc.h, 1)
                                v_rc.inc(1)

                            @blk.scalar
                            def _(s):
                                for e in range(nE):
                                    s.wait_ge(v_rm.h, e + 1)
                                    if e >= 2:
                                        s.wait_ge(v_rc.h, e - 1)
                                    ps = psS0 if e % 2 == 0 else psS1
                                    s.activation(big0[:, e * S:e * S + S], ps[:], AF.Exp,
                                                 bias=nmax8[:, e % 2:e % 2 + 1],
                                                 scale=0.125,
                                                 accum_out=rsum[:, e % 2:e % 2 + 1]
                                                 ).then_inc(a_exp.h, 1)
                                    a_exp.inc(1)
                                    if e == nE - 1:
                                        c_v.wait_all(s)
                                        s.wait_ge(t_s.h, 16)
                                        for j in (1, 3):
                                            s_vre.issue_idx(
                                                s, big1[:, j * 4 * S:(j + 1) * 4 * S],
                                                hap(v_h, j * 4 * P * S,
                                                    [[S, P], [P * S, 4], [1, S]]), j)

                    if upto >= 3:
                        with nc.psum_tensor("psO", [P, 2048], F32) as psO, \
                             nc.psum_tensor("psTA", [P, 2048], BF16) as psTA, \
                             nc.psum_tensor("psTB", [P, 2048], BF16) as psTB, \
                             nc.sbuf_tensor("atsl", [P, 2 * 2048], BF16) as atsl, \
                             nc.Block("p3", no_gpsimd_drain=True) as blk:

                            @blk.gpsimd
                            def _(g):
                                emit_quant_loads(g, QT_P2, QT_P2 + QT_P3)

                            @blk.sync
                            def _(sy):
                                for e_m in range(nE):
                                    emit_quant_spills(sy, QT_P2 + (e_m * QT_P3) // nE)
                                emit_quant_spills(sy, QT_P2 + QT_P3)

                            @blk.tensor
                            def _(t):
                                g_id.wait_all(t)

                                def trsp(e_m):
                                    pt = psTA if e_m % 2 == 0 else psTB
                                    if e_m >= 2:
                                        t.wait_ge(a_trs.h, e_m - 1)
                                    for f_k in range(nE):
                                        mm = t.transpose(pt[:, f_k * P:f_k * P + P],
                                                         big0[:, e_m * S + f_k * P:
                                                              e_m * S + f_k * P + P],
                                                         ident_bf[:])
                                        if f_k == nE - 1:
                                            mm.then_inc(t_tr.h, 1); t_tr.inc(1)

                                trsp(0)
                                trsp(1)
                                for e_m in range(nE):
                                    asl = atsl[:, (e_m % 2) * 2048:
                                               (e_m % 2) * 2048 + 2048]
                                    t.wait_ge(a_trs.h, e_m + 1)
                                    if e_m >= 1:
                                        t.wait_ge(a_oev.h, e_m)
                                    for f_k in range(nE):
                                        if e_m == 0 and f_k % 4 == 0:
                                            s_vre.wait_first_n(t, f_k // 4 + 1)
                                        for sn in range(4):
                                            mm = t.matmul(
                                                psO[:, sn * 512:sn * 512 + 512],
                                                asl[:, f_k * P:f_k * P + P],
                                                big1[:, f_k * S + sn * 512:
                                                     f_k * S + sn * 512 + 512],
                                                start=(f_k == 0), stop=(f_k == nE - 1))
                                            if f_k == nE - 1 and sn == 3:
                                                mm.then_inc(t_o.h, 1); t_o.inc(1)
                                    if e_m + 2 < nE:
                                        trsp(e_m + 2)

                            @blk.vector
                            def _(v):
                                for e_m in range(nE):
                                    emit_quant_vec(v, QT_P2 + (e_m * QT_P3) // nE,
                                                   QT_P2 + ((e_m + 1) * QT_P3) // nE)
                                emit_quant_flush(v)

                            @blk.scalar
                            def _(s):
                                s.wait_ge(v_rc.h, 16)
                                for e_m in range(nE):
                                    if e_m == 0:
                                        s.wait_ge(t_tr.h, 1)
                                        s.copy(atsl[:, 0:2048],
                                               psTA[:]).then_inc(a_trs.h, 1)
                                        a_trs.inc(1)
                                    if e_m + 1 < nE:
                                        s.wait_ge(t_tr.h, e_m + 2)
                                        if e_m >= 1:
                                            s.wait_ge(t_o.h, e_m)
                                        pt = psTA if (e_m + 1) % 2 == 0 else psTB
                                        s.copy(atsl[:, ((e_m + 1) % 2) * 2048:
                                                     ((e_m + 1) % 2) * 2048 + 2048],
                                               pt[:]).then_inc(a_trs.h, 1)
                                        a_trs.inc(1)
                                    s.wait_ge(t_o.h, e_m + 1)
                                    s.activation(big0[:, e_m * S:e_m * S + S],
                                                 psO[:], AF.Copy,
                                                 scale=recip_all[:, e_m:e_m + 1]
                                                 ).then_inc(a_oev.h, 1)
                                    a_oev.inc(1)

            with nc.sbuf_tensor("qout2", [P, 4 * 1024], BF16) as qout, \
                 nc.sbuf_tensor("qin2b", [P, 2 * 1024], F32) as qin2:
                if upto >= 4:
                    QT4_LO = QT_P2 + QT_P3
                    with contextlib.ExitStack() as st4:
                        ec4 = st4.enter_context
                        psy0 = ec4(nc.psum_tensor("psy0", [P, 512], F32))
                        psy1 = ec4(nc.psum_tensor("psy1", [P, 512], F32))
                        psy2 = ec4(nc.psum_tensor("psy2", [P, 512], F32))
                        psy3 = ec4(nc.psum_tensor("psy3", [P, 512], F32))
                        psy4 = ec4(nc.psum_tensor("psy4", [P, 512], F32))
                        psy5 = ec4(nc.psum_tensor("psy5", [P, 512], F32))
                        psT4 = ec4(nc.psum_tensor("psT4", [P, 2048], BF16))
                        xres = ec4(nc.sbuf_tensor("xres", [P, 2 * 2048], F32))
                        wobuf2 = ec4(nc.sbuf_tensor("wobuf2", [P, 16384], BF16))
                        x2t = ec4(nc.sbuf_tensor("x2t", [P, 2 * 2048], F32))
                        x2b = ec4(nc.sbuf_tensor("x2b", [P, 2 * 2048], BF16))
                        blk = ec4(nc.Block("p4", no_gpsimd_drain=True))

                        @blk.gpsimd
                        def _(g):
                            for j in range(2):
                                g_wo2.issue(
                                    g, wobuf2[:, j * 8192:(j + 1) * 8192],
                                    hap(wo, (8 + j * 4) * P * E,
                                        [[E, P], [P * E, 4], [1, E]]))
                            for s_m in range(nS):
                                if s_m >= 2:
                                    g.wait_ge(a_ln.h, s_m - 1)
                                s_x.issue(
                                    g, xres[:, (s_m % 2) * 2048:(s_m % 2) * 2048 + 2048],
                                    hap(x_tm, s_m * P * E, [[E, P], [1, E]]))
                                emit_quant_loads(
                                    g, QT4_LO + 2 * ((s_m * (QT_W1 - QT4_LO)) // nS // 2),
                                    QT4_LO + 2 * (((s_m + 1) * (QT_W1 - QT4_LO)) // nS // 2))

                        @blk.sync
                        def _(sy):
                            for s_m in range(nS):
                                emit_quant_spills(
                                    sy, QT4_LO + (s_m * (QT_W1 - QT4_LO)) // nS)
                            emit_quant_spills(sy, QT_W1)

                        @blk.tensor
                        def _(t):
                            g_wo.wait_all(t)
                            g_id.wait_all(t)

                            def x2_transposes(pm):
                                t.wait_ge(a_x2b.h, pm + 1)
                                if pm >= 1:
                                    t.wait_ge(a_tfx.h, pm)
                                xb = x2b[:, (pm % 2) * 2048:(pm % 2) * 2048 + 2048]
                                for d_k in range(nE):
                                    mm = t.transpose(psT4[:, d_k * P:d_k * P + P],
                                                     xb[:, d_k * P:d_k * P + P],
                                                     ident_bf[:])
                                    if d_k == nE - 1:
                                        mm.then_inc(t_tr4.h, 1); t_tr4.inc(1)

                            psy = [psy0, psy1, psy2, psy3, psy4, psy5]
                            t.wait_ge(a_oev.h, nE)
                            for s_m in range(nS):
                                if s_m >= 1:
                                    t.wait_ge(v_z.h, max(0, 4 * s_m - 2))
                                for e_k in range(nE):
                                    if s_m == 0 and e_k == 8:
                                        g_wo2.wait_first_n(t, 1)
                                    if s_m == 0 and e_k == 12:
                                        g_wo2.wait_first_n(t, 2)
                                    wsrc = wbuf if e_k < 8 else wobuf2
                                    wof = (e_k % 8) * E
                                    for nb in range(4):
                                        mm = t.matmul(
                                            psy[(4 * s_m + nb) % 6][:],
                                            big0[:, e_k * S + s_m * P:e_k * S + s_m * P + P],
                                            wsrc[:, wof + nb * 512:wof + nb * 512 + 512],
                                            start=(e_k == 0), stop=(e_k == nE - 1))
                                        if e_k == nE - 1 and nb == 3:
                                            mm.then_inc(t_y.h, 1); t_y.inc(1)
                                if s_m >= 2:
                                    x2_transposes(s_m - 2)
                            x2_transposes(nS - 2)
                            x2_transposes(nS - 1)
                            t.wait_ge(a_b1.h, 1)
                            t.wait_ge(a_tfx.h, 16)
                            t.transpose(psT4[:, 0:nF], b1tmpb[:], ident_bf[0:nF, 0:nF]
                                        ).then_inc(t_fin.h, 1)
                            t_fin.inc(1)

                        @blk.vector
                        def _(v):
                            c_wq.wait_first_n(v, (QT_P2 + QT_P3) // 2)
                            psy = [psy0, psy1, psy2, psy3, psy4, psy5]
                            for s_m in range(nS):
                                slot = x2t[:, (s_m % 2) * 2048:(s_m % 2) * 2048 + 2048]
                                lo = QT4_LO + (s_m * (QT_W1 - QT4_LO)) // nS
                                hi = QT4_LO + ((s_m + 1) * (QT_W1 - QT4_LO)) // nS
                                mid = (lo + hi) // 2
                                v.wait_ge(t_y.h, s_m + 1)
                                s_x.wait_item(v, s_m)
                                if s_m >= 2:
                                    c_x2.wait_item(v, s_m - 2)
                                    v.wait_ge(a_x2b.h, s_m - 1)
                                    v.wait_ge(a_ln.h, s_m - 1)
                                for nb in range(4):
                                    v.scalar_tensor_tensor(
                                        slot[:, nb * 512:nb * 512 + 512],
                                        psy[(4 * s_m + nb) % 6][:], 0.0,
                                        xres[:, (s_m % 2) * 2048 + nb * 512:
                                             (s_m % 2) * 2048 + nb * 512 + 512],
                                        op0=ALU.add, op1=ALU.add,
                                        accum_out=stats_acc[:, 4 * (s_m % 2) + nb:
                                                            4 * (s_m % 2) + nb + 1]
                                        ).then_inc(v_z.h, 1)
                                    v_z.inc(1)
                                emit_quant_vec(v, lo, mid)
                                v.wait_ge(a_ln.h, s_m + 1)
                                v.reciprocal(rec_t[:], ve_t[:]).then_inc(v_rp.h, 1)
                                v_rp.inc(1)
                                emit_quant_vec(v, mid, hi)
                                v.wait_ge(a_sq.h, s_m + 1)
                                v.tensor_scalar(slot, slot, rstd_t[:, 0:1], mr_t[:, 0:1],
                                                op0=ALU.mult, op1=ALU.add
                                                ).then_inc(v_x2.h, 1)
                                v_x2.inc(1)
                            emit_quant_flush(v)

                        @blk.scalar
                        def _(s):
                            s_misc.wait_item(s, 0)
                            s.copy(b1tmpb[:], b1tmp[:]).then_inc(a_b1.h, 1)
                            a_b1.inc(1)

                            def x2fm_evacs(pm):
                                s.wait_ge(t_tr4.h, pm + 1)
                                s.wait_ge(t_y.h, pm + 1)
                                s.copy(bass.AP(big0, pm * P, [[nE * S, P], [S, nE], [1, P]]),
                                       psT4[:]).then_inc(a_tfx.h, 1)
                                a_tfx.inc(1)

                            for s_m in range(nS):
                                sink = xres[:, (s_m % 2) * 2048:(s_m % 2) * 2048 + 2048]
                                slot = x2t[:, (s_m % 2) * 2048:(s_m % 2) * 2048 + 2048]
                                s.wait_ge(v_z.h, 4 * (s_m + 1))
                                if s_m >= 1:
                                    s.wait_ge(v_rp.h, s_m)
                                s.activation(sink, slot, AF.Square,
                                             accum_out=mv1[:, 1:2])
                                s.drain()
                                qb = 4 * (s_m % 2)
                                s.add(st6[:, 6:7], stats_acc[:, qb:qb + 1],
                                      stats_acc[:, qb + 1:qb + 2])
                                s.drain()
                                s.add(st6[:, 7:8], stats_acc[:, qb + 2:qb + 3],
                                      stats_acc[:, qb + 3:qb + 4])
                                s.drain()
                                s.add(mv1[:, 0:1], st6[:, 6:7], st6[:, 7:8])
                                s.drain()
                                s.mul(st6[:, 0:1], mv1[:, 0:1], 1.0 / 2048.0)
                                s.mul(st6[:, 1:2], mv1[:, 1:2], 1.0 / 2048.0)
                                s.drain()
                                s.square(st6[:, 2:3], st6[:, 0:1])
                                s.mul(st6[:, 4:5], st6[:, 0:1], -1.0)
                                s.drain()
                                s.activation(st6[:, 3:4], st6[:, 2:3], AF.Identity,
                                             scale=-1.0, bias=st6[:, 5:6])
                                s.drain()
                                s.add(ve_t[:], st6[:, 1:2], st6[:, 3:4]
                                      ).then_inc(a_ln.h, 1)
                                a_ln.inc(1)
                                s.wait_ge(v_rp.h, s_m + 1)
                                if s_m >= 1:
                                    s.wait_ge(v_x2.h, s_m)
                                s.activation(rstd_t[:], rec_t[:], AF.Sqrt)
                                s.drain()
                                s.mul(mr_t[:], st6[:, 4:5], rstd_t[:, 0:1]
                                      ).then_inc(a_sq.h, 1)
                                a_sq.inc(1)
                                s.wait_ge(v_x2.h, s_m + 1)
                                if s_m >= 2:
                                    s.wait_ge(t_tr4.h, s_m - 1)
                                s.copy(x2b[:, (s_m % 2) * 2048:(s_m % 2) * 2048 + 2048],
                                       x2t[:, (s_m % 2) * 2048:(s_m % 2) * 2048 + 2048]
                                       ).then_inc(a_x2b.h, 1)
                                a_x2b.inc(1)
                                c_x2.issue(s, hap(x2_h, s_m * P * E, [[E, P], [1, E]]),
                                           x2t[:, (s_m % 2) * 2048:(s_m % 2) * 2048 + 2048])
                                if s_m >= 2:
                                    x2fm_evacs(s_m - 2)
                            x2fm_evacs(nS - 2)
                            x2fm_evacs(nS - 1)
                            s.wait_ge(t_fin.h, 2)
                            s.copy(b1sb[:, 0:nF], psT4[:, 0:nF]).then_inc(a_b1.h, 1)
                            a_b1.inc(1)

                if upto >= 5:
                    with nc.psum_tensor("psH0", [P, 2048], F32) as psH0, \
                         nc.psum_tensor("psH1", [P, 2048], F32) as psH1, \
                         nc.sbuf_tensor("w1sl", [P, 2 * 2048], BF16) as w1sl, \
                         nc.sbuf_tensor("hstg", [P, 2 * 2048], BF16) as hstg, \
                         nc.Block("f1", no_gpsimd_drain=True) as blk:

                        @blk.gpsimd
                        def _(g):
                            emit_quant_loads(g, QT_W1, N_QT)

                        @blk.sync
                        def _(sy):
                            c_x2.wait_all(sy)
                            for f_m in range(nF):
                                if f_m == 0:
                                    c_wq.wait_first_n(sy, QT_W1 // 2)
                                if f_m >= 2:
                                    sy.wait_ge(t_h.h, f_m - 1)
                                s_w1sl.issue(
                                    sy, w1sl[:, (f_m % 2) * 2048:(f_m % 2) * 2048 + 2048],
                                    hap(w1q_h, f_m * P, [[FF, P], [P * FF, nE], [1, P]]))
                                if f_m >= 1:
                                    emit_quant_spills(sy, QT_W1 + 2 * (f_m - 1))
                            emit_quant_spills(sy, N_QT)

                        @blk.vector
                        def _(v):
                            for f_m in range(nF):
                                emit_quant_vec(v, QT_W1 + 2 * f_m, QT_W1 + 2 * f_m + 2)
                            emit_quant_flush(v)

                        @blk.tensor
                        def _(t):
                            for f_m in range(nF):
                                s_w1sl.wait_item(t, f_m)
                                if f_m >= 2:
                                    t.wait_ge(a_hh.h, f_m - 1)
                                ps = psH0 if f_m % 2 == 0 else psH1
                                for d_k in range(nE):
                                    for nb in range(4):
                                        mm = t.matmul(
                                            ps[:, nb * 512:nb * 512 + 512],
                                            w1sl[:, (f_m % 2) * 2048 + d_k * P:
                                                 (f_m % 2) * 2048 + d_k * P + P],
                                            big0[:, d_k * S + nb * 512:
                                                 d_k * S + nb * 512 + 512],
                                            start=(d_k == 0), stop=(d_k == nE - 1))
                                        if d_k == nE - 1 and nb == 3:
                                            mm.then_inc(t_h.h, 1); t_h.inc(1)

                        @blk.scalar
                        def _(s):
                            for f_m in range(nF):
                                s.wait_ge(t_h.h, f_m + 1)
                                if f_m >= 2:
                                    c_h.wait_item(s, f_m - 2)
                                ps = psH0 if f_m % 2 == 0 else psH1
                                s.activation(hstg[:, (f_m % 2) * 2048:
                                                  (f_m % 2) * 2048 + 2048],
                                             ps[:], AF.Gelu, bias=b1sb[:, f_m:f_m + 1]
                                             ).then_inc(a_hh.h, 1)
                                a_hh.inc(1)
                                s.wait_ge(a_hh.h, f_m + 1)
                                c_h.issue(s, hap(h_h, f_m * P * S, [[S, P], [1, S]]),
                                          hstg[:, (f_m % 2) * 2048:(f_m % 2) * 2048 + 2048])

        if upto >= 6:
            FG = 8
            nhalf = nS // 2
            with nc.psum_tensor("psF0", [P, 2048], F32) as psF0, \
                 nc.psum_tensor("psF1", [P, 2048], F32) as psF1, \
                 nc.sbuf_tensor("oacc", [P, 8 * 2048], F32) as oacc, \
                 nc.sbuf_tensor("w2g", [P, 2 * 8 * 2048], BF16) as w2g, \
                 nc.sbuf_tensor("hg", [P, 2 * 8 * 1024], BF16) as hg, \
                 nc.sbuf_tensor("lnsink", [P, 2048], F32) as lnsink, \
                 nc.sbuf_tensor("outt", [P, 2 * 2048], F32) as outt, \
                 nc.sbuf_tensor("b2bf", [1, E], BF16) as b2bf, \
                 nc.Block("f2", no_gpsimd_drain=True) as blk:

                @blk.gpsimd
                def _(g):
                    g_id.issue(g, b2bf[:], hap(b2, 0, [[E, 1], [1, E]]))

                @blk.sync
                def _(sy):
                    c_wq.wait_all(sy)
                    c_h.wait_all(sy)

                    def grp_loads(gi):
                        half, g = gi // FG, gi % FG
                        if gi >= 2:
                            sy.wait_ge(t_f.h, (gi - 1) * nhalf)
                        c_h.wait_first_n(sy, 8 * (g + 1))
                        slot = gi % 2
                        if gi == 0:
                            s_w2g.issue_idx(
                                sy, w2g[:, 0:8192],
                                hap(w2q_h, 0, [[E, P], [P * E, 4], [1, E]]), 0)
                        else:
                            s_w2g.issue_idx(
                                sy, w2g[:, slot * 16384:slot * 16384 + 16384],
                                hap(w2q_h, (g * 8) * P * E,
                                    [[E, P], [P * E, 8], [1, E]]), gi)
                        if gi > 0:
                            s_hg.issue_idx(
                                sy, hg[:, slot * 8192:slot * 8192 + 8192],
                                hap(h_h, (g * 8) * P * S + half * 1024,
                                    [[S, P], [P * S, 8], [1, 1024]]), gi)

                    grp_loads(0)
                    c_x2.wait_all(sy)
                    for j in range(nhalf):
                        s_x2i.issue(sy, oacc[:, j * 2048:j * 2048 + 2048],
                                    hap(x2_h, j * P * E, [[E, P], [1, E]]))
                    for gi in range(1, FG):
                        grp_loads(gi)
                    grp_loads(FG)
                    for j in range(nhalf):
                        sy.wait_ge(v_ot.h, j + 1)
                        s_x2i.issue(sy, oacc[:, j * 2048:j * 2048 + 2048],
                                    hap(x2_h, (nhalf + j) * P * E, [[E, P], [1, E]]))
                    for gi in range(FG + 1, 2 * FG):
                        grp_loads(gi)

                @blk.tensor
                def _(t):
                    g_id.wait_all(t)
                    for half in range(2):
                        for g in range(FG):
                            gi = half * FG + g
                            slot = gi % 2
                            s_w2g.wait_item(t, gi)
                            s_hg.wait_item(t, gi)
                            for j in range(nhalf):
                                ti = gi * nhalf + j
                                if ti >= 2:
                                    t.wait_ge(v_fa.h, ti - 1)
                                ps = psF0 if ti % 2 == 0 else psF1
                                if g == 0:
                                    for nb in range(4):
                                        t.matmul(ps[:, nb * 512:nb * 512 + 512],
                                                 ones_bf[:, 0:P],
                                                 b2bf[:, nb * 512:nb * 512 + 512],
                                                 start=True, stop=False)
                                for f_k in range(8):
                                    if gi == 0 and j == 0 and f_k == 4:
                                        s_misc.wait_item(t, 1)
                                    for nb in range(4):
                                        mm = t.matmul(
                                            ps[:, nb * 512:nb * 512 + 512],
                                            hg[:, slot * 8192 + f_k * 1024 + j * P:
                                                slot * 8192 + f_k * 1024 + j * P + P],
                                            w2g[:, slot * 16384 + f_k * E + nb * 512:
                                                slot * 16384 + f_k * E + nb * 512 + 512],
                                            start=(g != 0 and f_k == 0),
                                            stop=(f_k == 7))
                                        if f_k == 7 and nb == 3:
                                            mm.then_inc(t_f.h, 1); t_f.inc(1)

                @blk.vector
                def _(v):
                    def ln2_recip(hj):
                        v.wait_ge(a_ln2.h, hj + 1)
                        if hj >= 1:
                            v.wait_ge(a_sq2.h, hj)
                        v.reciprocal(rec_t[:], ve_t[:]).then_inc(v_rp2.h, 1)
                        v_rp2.inc(1)

                    def ln2_affine(hj):
                        j = hj % nhalf
                        v.wait_ge(a_sq2.h, hj + 1)
                        if hj >= 2:
                            c_y.wait_item(v, hj - 2)
                        v.tensor_scalar(outt[:, (hj % 2) * 2048:
                                            (hj % 2) * 2048 + 2048],
                                        oacc[:, j * 2048:j * 2048 + 2048],
                                        rstd_t[:, 0:1], mr_t[:, 0:1],
                                        op0=ALU.mult, op1=ALU.add
                                        ).then_inc(v_ot.h, 1)
                        v_ot.inc(1)

                    for half in range(2):
                        for g in range(FG):
                            gi = half * FG + g
                            for j in range(nhalf):
                                ti = gi * nhalf + j
                                v.wait_ge(t_f.h, ti + 1)
                                if g == 0:
                                    s_x2i.wait_item(v, half * nhalf + j)
                                ps = psF0 if ti % 2 == 0 else psF1
                                if g == FG - 1:
                                    v.scalar_tensor_tensor(
                                        oacc[:, j * 2048:j * 2048 + 2048], ps[:],
                                        0.0, oacc[:, j * 2048:j * 2048 + 2048],
                                        op0=ALU.add, op1=ALU.add,
                                        accum_out=stats_acc[:, 8 + j:9 + j]
                                        ).then_inc(v_fa.h, 1)
                                else:
                                    v.tensor_tensor(
                                        oacc[:, j * 2048:j * 2048 + 2048], ps[:],
                                        oacc[:, j * 2048:j * 2048 + 2048],
                                        op=ALU.add).then_inc(v_fa.h, 1)
                                v_fa.inc(1)
                                if g == FG - 1:
                                    if j >= 4:
                                        ln2_affine(half * nhalf + j - 4)
                                    if j >= 2:
                                        ln2_recip(half * nhalf + j - 2)
                        for j in range(nhalf, nhalf + 4):
                            ln2_affine(half * nhalf + j - 4)
                            if j - 2 < nhalf:
                                ln2_recip(half * nhalf + j - 2)

                @blk.scalar
                def _(s):
                    c_h.wait_all(s)
                    c_wq.wait_all(s)
                    s_hg.issue_idx(
                        s, hg[:, 0:8192],
                        hap(h_h, 0, [[S, P], [P * S, 8], [1, 1024]]), 0)
                    s_misc.issue_idx(
                        s, w2g[:, 8192:16384],
                        hap(w2q_h, 4 * P * E, [[E, P], [P * E, 4], [1, E]]), 1)
                    nm_col = [st6[:, 4 + 2 * (k % 2):5 + 2 * (k % 2)] for k in range(2)]

                    def ln2_stats(hj):
                        half, j = hj // nhalf, hj % nhalf
                        strip = oacc[:, j * 2048:j * 2048 + 2048]
                        last_ti = ((half + 1) * FG - 1) * nhalf + j
                        s.wait_ge(v_fa.h, last_ti + 1)
                        if hj >= 1:
                            s.wait_ge(v_rp2.h, hj)
                        s.activation(lnsink[:], strip, AF.Square,
                                     accum_out=mv1[:, 1:2])
                        s.drain()
                        s.mul(st6[:, 0:1], stats_acc[:, 8 + j:9 + j], 1.0 / 2048.0)
                        s.mul(st6[:, 1:2], mv1[:, 1:2], 1.0 / 2048.0)
                        s.drain()
                        s.square(st6[:, 2:3], st6[:, 0:1])
                        s.mul(nm_col[hj % 2][:], st6[:, 0:1], -1.0)
                        s.drain()
                        s.activation(st6[:, 3:4], st6[:, 2:3], AF.Identity,
                                     scale=-1.0, bias=st6[:, 5:6])
                        s.drain()
                        s.add(ve_t[:], st6[:, 1:2], st6[:, 3:4]
                              ).then_inc(a_ln2.h, 1)
                        a_ln2.inc(1)

                    def ln2_sqrt(hj):
                        s.wait_ge(v_rp2.h, hj + 1)
                        if hj >= 1:
                            s.wait_ge(v_ot.h, hj)
                        s.activation(rstd_t[:], rec_t[:], AF.Sqrt)
                        s.drain()
                        s.mul(mr_t[:], nm_col[hj % 2][:], rstd_t[:, 0:1]
                              ).then_inc(a_sq2.h, 1)
                        a_sq2.inc(1)

                    def ln2_spill(hj):
                        s.wait_ge(v_ot.h, hj + 1)
                        c_y.issue(s, hap(y, hj * P * E, [[E, P], [1, E]]),
                                  outt[:, (hj % 2) * 2048:(hj % 2) * 2048 + 2048])

                    for half in range(2):
                        for k in range(nhalf):
                            hj = half * nhalf + k
                            ln2_stats(hj)
                            if k >= 1:
                                ln2_sqrt(hj - 1)
                            if k >= 2:
                                ln2_spill(hj - 2)
                        ln2_sqrt(half * nhalf + nhalf - 1)
                        ln2_spill(half * nhalf + nhalf - 2)
                        ln2_spill(half * nhalf + nhalf - 1)
                    c_y.wait_all(s)

        with nc.Block("fin", no_gpsimd_drain=True) as blk:
            @blk.sync
            def _(sy):
                for st_ in (g_xfm, g_id, g_wq, g_wk, g_wo, g_wo2, s_wv,
                            s_stats, s_qsl, s_vre, s_osl, s_x, s_x2i, s_w1sl,
                            s_w2g, s_hg, s_qin, s_misc,
                            c_q, c_v, c_o, c_x2, c_h, c_wq, c_y):
                    st_.wait_all(sy)

    return nc


_NC_CACHE = {}


def _get_nc(upto=6, debug=False):
    key = (upto, debug)
    if key not in _NC_CACHE:
        _NC_CACHE[key] = build_nc(upto, debug)
    return _NC_CACHE[key]


def make_in_maps(x, Wq, Wk, Wv, Wo, w1T, b1v, w2T, b2v):
    ident = np.eye(P, dtype=np.float32)
    in_maps = []
    for c in range(x.shape[0]):
        xc = np.ascontiguousarray(x[c])
        in_maps.append({
            "x_tm": xc,
            "x_fm": np.ascontiguousarray(xc.T),
            "wq": Wq, "wk": Wk, "wv": Wv, "wo": Wo,
            "w1t": w1T, "b1": b1v, "w2t": w2T, "b2": b2v,
            "ident_in": ident,
        })
    return in_maps


def kernel(L0_attn_x, L0_attn_Wq, L0_attn_Wk, L0_attn_Wv, L0_attn_Wo,
           L0_ffn_w1, L0_ffn_b1, L0_ffn_w2, L0_ffn_b2):
    x = np.ascontiguousarray(np.asarray(L0_attn_x, dtype=np.float32))
    Wq = np.ascontiguousarray(np.asarray(L0_attn_Wq, dtype=np.float32))
    Wk = np.ascontiguousarray(np.asarray(L0_attn_Wk, dtype=np.float32))
    Wv = np.ascontiguousarray(np.asarray(L0_attn_Wv, dtype=np.float32))
    Wo = np.ascontiguousarray(np.asarray(L0_attn_Wo, dtype=np.float32))
    w1T = np.ascontiguousarray(np.asarray(L0_ffn_w1, dtype=np.float32).T)
    w2T = np.ascontiguousarray(np.asarray(L0_ffn_w2, dtype=np.float32).T)
    b1v = np.ascontiguousarray(np.asarray(L0_ffn_b1, dtype=np.float32))
    b2v = np.ascontiguousarray(np.asarray(L0_ffn_b2, dtype=np.float32))

    nc = _get_nc()
    in_maps = make_in_maps(x, Wq, Wk, Wv, Wo, w1T, b1v, w2T, b2v)
    res = run_bass_kernel_spmd(nc, in_maps, core_ids=list(range(8)))
    return np.stack([np.asarray(r["y"]) for r in res.results], axis=0)


# version 20
# speedup vs baseline: 1.0314x; 1.0088x over previous
import numpy as np
import concourse.bass as bass
import concourse.mybir as mybir
from concourse.bass_utils import run_bass_kernel_spmd

F32 = mybir.dt.float32
BF16 = mybir.dt.bfloat16
AF = mybir.ActivationFunctionType
ALU = mybir.AluOpType
AX = mybir.AxisListType.X

P = 128
S = 2048
E = 2048
FF = 8192
nS = S // P
nE = E // P
nF = FF // P
W_ELEMS = E * FF


C_HI = float(np.float32(0.7))
C_LO = float(np.float32(np.float64(0.7) - np.float64(np.float32(0.7))))
_g = np.float32(C_HI) * np.float32(4097.0)
_d = np.float32(_g) - np.float32(C_HI)
CH1 = float(np.float32(_g) - np.float32(_d))
CH2 = float(np.float32(C_HI) - np.float32(CH1))
CSPLIT = 12582912.0
INV24 = float(2.0 ** -24)

N_QT = 256
ST_CUT = 232
QT_P2 = 32
QT_P3 = 64
QT_W1 = 128


class Ctr:

    def __init__(self, stack, nc, name):
        self.h = stack.enter_context(nc.semaphore(name))
        self.v = 0

    def inc(self, n=1):
        self.v += n
        return self.v


class DStream:

    def __init__(self, stack, nc, name, nsem=2):
        self.sems = [Ctr(stack, nc, "%s%d" % (name, i)) for i in range(nsem)]
        self.n = 0

    def issue(self, eng, out, in_):
        i = self.n
        self.n += 1
        s = self.sems[i % len(self.sems)]
        eng.dma_start(out=out, in_=in_).then_inc(s.h, 16)
        s.inc(16)
        return i

    def issue_idx(self, eng, out, in_, i):
        s = self.sems[i % len(self.sems)]
        eng.dma_start(out=out, in_=in_).then_inc(s.h, 16)
        s.inc(16)
        return i

    def wait_item(self, eng, i):
        k = len(self.sems)
        eng.wait_ge(self.sems[i % k].h, 16 * (i // k + 1))

    def wait_first_n(self, eng, n):
        k = len(self.sems)
        for j, s in enumerate(self.sems):
            cnt = (n - j + k - 1) // k
            if cnt > 0:
                eng.wait_ge(s.h, 16 * cnt)

    def wait_all(self, eng):
        for s in self.sems:
            if s.v > 0:
                eng.wait_ge(s.h, s.v)


def build_nc(upto=6, debug=False):
    import contextlib

    nc = bass.Bass("TRN2", target_bir_lowering=False)
    scratch_kind = "ExternalOutput" if debug else "Internal"

    x_tm = nc.dram_tensor("x_tm", [S, E], F32, kind="ExternalInput")
    x_fm = nc.dram_tensor("x_fm", [E, S], F32, kind="ExternalInput")
    wq = nc.dram_tensor("wq", [E, E], F32, kind="ExternalInput")
    wk = nc.dram_tensor("wk", [E, E], F32, kind="ExternalInput")
    wv = nc.dram_tensor("wv", [E, E], F32, kind="ExternalInput")
    wo = nc.dram_tensor("wo", [E, E], F32, kind="ExternalInput")
    w1t = nc.dram_tensor("w1t", [E, FF], F32, kind="ExternalInput")
    b1 = nc.dram_tensor("b1", [FF], F32, kind="ExternalInput")
    w2t = nc.dram_tensor("w2t", [FF, E], F32, kind="ExternalInput")
    b2 = nc.dram_tensor("b2", [E], F32, kind="ExternalInput")
    ident_in = nc.dram_tensor("ident_in", [P, P], F32, kind="ExternalInput")

    y = nc.dram_tensor("y", [S, E], F32, kind="ExternalOutput")

    q_h = nc.dram_tensor("q_h", [S, E], BF16, kind=scratch_kind)
    v_h = nc.dram_tensor("v_h", [E, S], BF16, kind=scratch_kind)
    x2_h = nc.dram_tensor("x2_h", [S, E], F32, kind=scratch_kind)
    h_h = nc.dram_tensor("h_h", [FF, S], BF16, kind=scratch_kind)
    w1q_h = nc.dram_tensor("w1q_h", [E, FF], BF16, kind=scratch_kind)
    w2q_h = nc.dram_tensor("w2q_h", [FF, E], BF16, kind=scratch_kind)

    def hap(t, off, dims):
        return bass.AP(t, off, dims)

    with contextlib.ExitStack() as stx:
        sem = lambda name: Ctr(stx, nc, name)
        strm = lambda name, k=2: DStream(stx, nc, name, k)

        g_xfm = strm("g_xfm"); g_id = strm("g_id", 1)
        g_wq = strm("g_wq", 4); g_wk = strm("g_wk", 4)
        g_wo = strm("g_wo"); g_wo2 = strm("g_wo2")
        s_stats = strm("s_st", 5); s_qsl = strm("s_qsl"); s_vre = strm("s_vre", 4)
        s_osl = strm("s_osl"); s_x = strm("s_x"); s_x2i = strm("s_x2i", 8)
        s_w1sl = strm("s_w1sl"); s_w2g = strm("s_w2g"); s_hg = strm("s_hg")
        s_qin = strm("s_qin"); s_misc = strm("s_msc", 3); s_wv = strm("s_wv", 1)
        c_q = strm("c_q"); c_v = strm("c_v"); c_o = strm("c_o")
        c_x2 = strm("c_x2"); c_h = strm("c_h"); c_wq = strm("c_wq")
        c_y = strm("c_y")
        t_q = sem("t_q"); t_s = sem("t_s"); t_tr = sem("t_tr"); t_o = sem("t_o")
        t_y = sem("t_y"); t_tr4 = sem("t_tr4"); t_h = sem("t_h"); t_f = sem("t_f")
        t_fin = sem("t_fin")
        v_ev = sem("v_ev"); v_vv = sem("v_vv"); v_wacc = sem("v_wacc")
        v_rm = sem("v_rm"); v_rc = sem("v_rc"); v_qt = sem("v_qt")
        v_z = sem("v_z"); v_x2 = sem("v_x2"); v_fa = sem("v_fa"); v_ot = sem("v_ot")
        a_st = sem("a_st"); a_exp = sem("a_exp"); a_tcp = sem("a_tcp")
        a_oev = sem("a_oev"); a_trs = sem("a_trs"); a_x2b = sem("a_x2b")
        a_tfx = sem("a_tfx"); a_sq = sem("a_sq"); a_hh = sem("a_hh")
        a_sq2 = sem("a_sq2"); a_b1 = sem("a_b1")
        g_bc = sem("g_bc"); g_qm = sem("g_qm"); a_wv = sem("a_wv")
        a_ln = sem("a_ln"); v_rp = sem("v_rp"); a_ln2 = sem("a_ln2"); v_rp2 = sem("v_rp2")

        ident_bf = stx.enter_context(nc.sbuf_tensor("ident_bf", [P, P], BF16))
        ones_f = stx.enter_context(nc.sbuf_tensor("ones_f", [P, 1], F32))
        ones_bf = stx.enter_context(nc.sbuf_tensor("ones_bf", [1, P], BF16))
        ones_f2 = stx.enter_context(nc.sbuf_tensor("ones_f2", [1, P], F32))
        b1tmpb = stx.enter_context(nc.sbuf_tensor("b1tmpb", [nF, P], BF16))
        stats_acc = stx.enter_context(nc.sbuf_tensor("stats_acc", [P, 256], F32))
        wacc = stx.enter_context(nc.sbuf_tensor("wacc", [P, 4], F32))
        tvals = stx.enter_context(nc.sbuf_tensor("tvals", [1, 4], F32))
        tbc = stx.enter_context(nc.sbuf_tensor("tbc", [P, 8], F32))
        tq = stx.enter_context(nc.sbuf_tensor("tq", [1, 8], F32))
        thw = stx.enter_context(nc.sbuf_tensor("thw", [1, 12], F32))
        recip_all = stx.enter_context(nc.sbuf_tensor("recip_all", [P, nE], F32))
        nmax8 = stx.enter_context(nc.sbuf_tensor("nmax8", [P, 2], F32))
        rsum = stx.enter_context(nc.sbuf_tensor("rsum", [P, 2], F32))
        b1sb = stx.enter_context(nc.sbuf_tensor("b1sb", [P, nF], F32))
        b1tmp = stx.enter_context(nc.sbuf_tensor("b1tmp", [nF, P], F32))
        mv1 = stx.enter_context(nc.sbuf_tensor("mv1", [P, 2], F32))
        st6 = stx.enter_context(nc.sbuf_tensor("st6", [P, 24], F32))
        ve_t = stx.enter_context(nc.sbuf_tensor("ve_t", [P, 1], F32))
        rec_t = stx.enter_context(nc.sbuf_tensor("rec_t", [P, 1], F32))
        rstd_t = stx.enter_context(nc.sbuf_tensor("rstd_t", [P, 1], F32))
        mr_t = stx.enter_context(nc.sbuf_tensor("mr_t", [P, 1], F32))

        def w_tile_src(i):
            if i < 128:
                d_k, jj = i // 8, i % 8
                return hap(w1t, d_k * P * FF + jj * 1024, [[FF, P], [1, 1024]])
            j = i - 128
            f_k, jj = j // 2, j % 2
            return hap(w2t, f_k * P * E + jj * 1024, [[E, P], [1, 1024]])

        def wq_tile_dst(i):
            if i < 128:
                d_k, jj = i // 8, i % 8
                return hap(w1q_h, d_k * P * FF + jj * 1024, [[FF, P], [1, 1024]])
            j = i - 128
            f_k, jj = j // 2, j % 2
            return hap(w2q_h, f_k * P * E + jj * 1024, [[E, P], [1, 1024]])

        with nc.sbuf_tensor("big0", [P, nE * S], BF16) as big0, \
             nc.sbuf_tensor("qin", [P, 2 * 1024], F32) as qin, \
             nc.sbuf_tensor("qmask", [P, 1024], F32) as qmask, \
             nc.sbuf_tensor("qmask2", [P, 1024], F32) as qmask2, \
             nc.sbuf_tensor("wbuf", [P, 16384], BF16) as wbuf:

            qs_spilled = [0]

            def w_pair_src(p):
                i = 2 * p
                if i < 128:
                    d_k, jj = i // 8, i % 8
                    return hap(w1t, d_k * P * FF + jj * 1024, [[FF, P], [1, 2048]])
                j = i - 128
                f_k = j // 2
                return hap(w2t, f_k * P * E, [[E, P], [1, 2048]])

            def wq_pair_dst(p):
                i = 2 * p
                if i < 128:
                    d_k, jj = i // 8, i % 8
                    return hap(w1q_h, d_k * P * FF + jj * 1024, [[FF, P], [1, 2048]])
                j = i - 128
                f_k = j // 2
                return hap(w2q_h, f_k * P * E, [[E, P], [1, 2048]])

            def emit_quant_spills(eng, upto_tile):
                while (qs_spilled[0] + 1) * 2 <= upto_tile:
                    p = qs_spilled[0]
                    eng.wait_ge(v_qt.h, 2 * p + 2)
                    c_wq.issue(eng, wq_pair_dst(p),
                               qout[:, (p % 2) * 2048:(p % 2) * 2048 + 2048])
                    qs_spilled[0] += 1

            def emit_quant_loads(g, lo, hi):
                for p in range(lo // 2, hi // 2):
                    if p >= 2:
                        g.wait_ge(g_qm.h, 2 * (p - 1))
                    dst = qin if p % 2 == 0 else qin2
                    s_qin.issue(g, dst[:], w_pair_src(p))

            q_pend = []

            def emit_quant_tt(v):
                i, mka, need = q_pend.pop(0)
                qo = qout[:, (i % 4) * 1024:(i % 4) * 1024 + 1024]
                v.wait_ge(g_qm.h, need)
                v.tensor_tensor(qo, mka[:], qo, op=ALU.subtract).then_inc(v_qt.h, 1)
                v_qt.inc(1)

            def emit_quant_flush(v):
                while q_pend:
                    emit_quant_tt(v)

            def emit_quant_vec(v, lo, hi):
                for i in range(lo, hi):
                    j = 0 if i < 128 else 1
                    s_qin.wait_item(v, i // 2)
                    if i >= 2:
                        v.wait_ge(v_qt.h, i - 1)
                    qbuf = qin if (i // 2) % 2 == 0 else qin2
                    src = qbuf[:, (i % 2) * 1024:(i % 2) * 1024 + 1024]
                    mka = qmask if i % 2 == 0 else qmask2
                    v.tensor_scalar(mka[:], src, tbc[:, 4 * j:4 * j + 1],
                                    tbc[:, 4 * j + 1:4 * j + 2],
                                    op0=ALU.subtract, op1=ALU.is_ge)
                    if i >= 4:
                        c_wq.wait_item(v, i // 2 - 2)
                    v.tensor_scalar(qout[:, (i % 4) * 1024:(i % 4) * 1024 + 1024],
                                    src, tbc[:, 4 * j + 2:4 * j + 3],
                                    tbc[:, 4 * j + 3:4 * j + 4],
                                    op0=ALU.subtract, op1=ALU.is_le
                                    ).then_inc(g_qm.h, 1)
                    g_qm.inc(1)
                    q_pend.append((i, mka, g_qm.v))
                    if len(q_pend) > 1:
                        emit_quant_tt(v)


            with nc.sbuf_tensor("big1", [P, nE * S], BF16) as big1:

                with nc.psum_tensor("psA", [P, 2048], F32) as psA, \
                     nc.psum_tensor("psB", [P, 2048], F32) as psB, \
                     nc.sbuf_tensor("spillb", [P, 4096], BF16) as spillb, \
                     nc.sbuf_tensor("sbin", [P, 5 * 1024], F32) as sbin, \
                     nc.Block("p1", no_gpsimd_drain=True) as blk:

                    def qk_view(r):
                        t = psA if r < 4 else psB
                        return t[:, (r % 4) * 512:(r % 4) * 512 + 512]

                    @blk.gpsimd
                    def _(g):
                        g_id.issue(g, ident_bf[:], hap(ident_in, 0, [[P, P], [1, P]]))
                        for j in range(4):
                            g_xfm.issue(
                                g, big0[:, j * 4 * S:(j + 1) * 4 * S],
                                hap(x_fm, j * 4 * P * S, [[S, P], [P * S, 4], [1, S]]))
                        for i in range(8):
                            w = wq if i < 4 else wk
                            st_w = g_wq if i < 4 else g_wk
                            nb = i % 4
                            if i >= 2:
                                g.wait_ge(t_q.h, 16 * (i - 1))
                            for h in range(2):
                                st_w.issue_idx(
                                    g, wbuf[:, (i % 2) * 8192 + h * 4096:
                                            (i % 2) * 8192 + h * 4096 + 4096],
                                    hap(w, nb * 512 + h * 8 * P * E,
                                        [[E, P], [P * E, 8], [1, 512]]),
                                    2 * (i % 4) + h)
                        g.wait_ge(t_q.h, 144)
                        for j in range(2):
                            g_wo.issue(
                                g, wbuf[:, j * 8192:(j + 1) * 8192],
                                hap(wo, j * 4 * P * E, [[E, P], [P * E, 4], [1, E]]))

                    @blk.sync
                    def _(sy):
                        for i in (list(range(0, ST_CUT, 2))
                                  + list(range(ST_CUT, N_QT))):
                            if i >= 5:
                                sy.wait_ge(a_st.h, i - 4)
                            s_stats.issue_idx(
                                sy, sbin[:, (i % 5) * 1024:(i % 5) * 1024 + 1024],
                                w_tile_src(i), i)
                        s_misc.issue(sy, b1tmp[:], hap(b1, 0, [[P, nF], [1, P]]))

                    @blk.tensor
                    def _(t):
                        g_xfm.wait_all(t)
                        for i in range(8):
                            st_w = g_wq if i < 4 else g_wk
                            st_w.wait_item(t, 2 * (i % 4))
                            half = wbuf[:, (i % 2) * 8192:(i % 2) * 8192 + 8192]
                            for s_m in range(nS):
                                ti = i * nS + s_m
                                if ti >= 8:
                                    t.wait_ge(v_ev.h, ti - 7)
                                pv = qk_view(ti % 8)
                                for e_k in range(nE):
                                    if s_m == 0 and e_k == 8:
                                        st_w.wait_item(t, 2 * (i % 4) + 1)
                                    mm = t.matmul(
                                        pv,
                                        big0[:, e_k * S + s_m * P:e_k * S + s_m * P + P],
                                        half[:, e_k * 512:e_k * 512 + 512],
                                        start=(e_k == 0), stop=(e_k == nE - 1))
                                    if e_k == nE - 1:
                                        mm.then_inc(t_q.h, 1); t_q.inc(1)
                        t.wait_ge(v_ev.h, 128)
                        for f_m in range(nE):
                            t.wait_ge(a_wv.h, f_m + 1)
                            if f_m >= 2:
                                t.wait_ge(v_vv.h, f_m - 1)
                            ps = psA if f_m % 2 == 0 else psB
                            for e_k in range(nE):
                                for sn in range(4):
                                    mm = t.matmul(
                                        ps[:, sn * 512:sn * 512 + 512],
                                        wbuf[:, (f_m % 8) * 2048 + e_k * P:
                                             (f_m % 8) * 2048 + e_k * P + P],
                                        big0[:, e_k * S + sn * 512:
                                             e_k * S + sn * 512 + 512],
                                        start=(e_k == 0), stop=(e_k == nE - 1))
                                    if e_k == nE - 1 and sn == 3:
                                        mm.then_inc(t_q.h, 1); t_q.inc(1)
                        t.wait_ge(v_vv.h, 15)
                        t.wait_ge(v_wacc.h, 1)
                        t.matmul(psA[0:1, 0:4], ones_f[:, 0:1], wacc[:, 0:4],
                                 start=True, stop=True).then_inc(t_fin.h, 1)
                        t_fin.inc(1)

                    @blk.vector
                    def _(v):
                        v.memset(ones_f[:], 1.0)
                        v.memset(ones_bf[:], 1.0)
                        v.memset(ones_f2[:], 1.0)
                        v.memset(st6[:, 5:6], 1e-5)
                        for ti in range(128):
                            v.wait_ge(t_q.h, ti + 1)
                            i, s_m = ti // nS, ti % nS
                            pv = qk_view(ti % 8)
                            if i < 4:
                                qtr = s_m // 4
                                hidx = i * 4 + qtr
                                slot = hidx % 2
                                if hidx >= 2:
                                    c_q.wait_item(v, hidx - 2)
                                v.tensor_copy(
                                    spillb[:, slot * 2048 + (s_m % 4) * 512:
                                           slot * 2048 + (s_m % 4) * 512 + 512],
                                    pv).then_inc(v_ev.h, 1)
                            else:
                                nb = i - 4
                                v.tensor_copy(
                                    big1[:, s_m * S + nb * 512:s_m * S + nb * 512 + 512],
                                    pv).then_inc(v_ev.h, 1)
                            v_ev.inc(1)
                        for f_m in range(nE):
                            v.wait_ge(t_q.h, 128 + f_m + 1)
                            ps = psA if f_m % 2 == 0 else psB
                            if f_m < 2:
                                c_q.wait_first_n(v, 16)
                            if f_m >= 2:
                                c_v.wait_item(v, f_m - 2)
                            v.tensor_copy(spillb[:, (f_m % 2) * 2048:
                                                 (f_m % 2) * 2048 + 2048],
                                          ps[:]).then_inc(v_vv.h, 1)
                            v_vv.inc(1)
                        v.wait_ge(a_st.h, N_QT)
                        thr_hi = sbin[:, 0:128]
                        thr_lo = sbin[:, 1024:1152]
                        for j in range(2):
                            blkc = stats_acc[:, 128 * j:128 * j + 128]
                            v.tensor_scalar(thr_hi[:], blkc, CSPLIT, CSPLIT,
                                            op0=ALU.add, op1=ALU.subtract)
                            v.drain()
                            v.tensor_tensor(thr_lo[:], blkc, thr_hi[:],
                                            op=ALU.subtract)
                            v.drain()
                            v.reduce_sum(wacc[:, 2 * j:2 * j + 1], thr_hi[:], axis=AX)
                            mm_last = v.reduce_sum(wacc[:, 2 * j + 1:2 * j + 2],
                                                   thr_lo[:], axis=AX)
                            v.drain()
                            if j == 1:
                                mm_last.then_inc(v_wacc.h, 1)
                                v_wacc.inc(1)

                    @blk.scalar
                    def _(s):
                        def q_spill(hidx):
                            i, qtr = hidx // 4, hidx % 4
                            slot = hidx % 2
                            s.wait_ge(v_ev.h, i * nS + qtr * 4 + 4)
                            c_q.issue(
                                s,
                                hap(q_h, (qtr * 4 * P) * E + i * 512,
                                    [[E, P], [P * E, 4], [1, 512]]),
                                bass.AP(spillb, slot * 2048,
                                        [[4096, P], [512, 4], [1, 512]]))

                        spill_at = {}
                        for hidx in range(16):
                            thr = min(2 * (16 * (hidx // 4) + (hidx % 4) * 4 + 4),
                                      N_QT - 1)
                            spill_at.setdefault(thr, []).append(hidx)
                        s_wv.issue(s, qin[:, 0:2048],
                                   hap(wv, 0, [[E, P], [P * E, nE], [1, P]]))
                        s_stats.issue_idx(s, sbin[:, 1024:2048], w_tile_src(1), 1)
                        s_stats.issue_idx(s, sbin[:, 3072:4096], w_tile_src(3), 3)

                        def v_cast(f_m):
                            s_wv.wait_item(s, f_m)
                            s.wait_ge(t_q.h, 112 if f_m < 4 else
                                      (128 if f_m < 8 else 128 + (f_m - 8) + 1))
                            s.copy(wbuf[:, (f_m % 8) * 2048:(f_m % 8) * 2048 + 2048],
                                   qin[:, 0:2048]).then_inc(a_wv.h, 1)
                            a_wv.inc(1)
                            s.wait_ge(a_wv.h, a_wv.v)
                            if f_m < nE - 1:
                                s_wv.issue(s, qin[:, 0:2048],
                                           hap(wv, (f_m + 1) * P,
                                               [[E, P], [P * E, nE], [1, P]]))
                            if f_m >= 1:
                                s.wait_ge(v_vv.h, f_m)
                                c_v.issue(s, hap(v_h, (f_m - 1) * P * S,
                                                 [[S, P], [1, S]]),
                                          spillb[:, ((f_m - 1) % 2) * 2048:
                                                 ((f_m - 1) % 2) * 2048 + 2048])

                        def emit_stat(i):
                            s_stats.wait_item(s, i)
                            slot = sbin[:, (i % 5) * 1024:(i % 5) * 1024 + 1024]
                            s.activation(slot, slot,
                                         AF.Abs, accum_out=stats_acc[:, i:i + 1]
                                         ).then_inc(a_st.h, 1)
                            a_st.inc(1)
                            if i % 2 == 1 and i + 4 < ST_CUT:
                                s.wait_ge(a_st.h, i)
                                s_stats.issue_idx(s, sbin[:, ((i + 4) % 5) * 1024:
                                                          ((i + 4) % 5) * 1024 + 1024],
                                                  w_tile_src(i + 4), i + 4)
                            for hidx in spill_at.get(i, []):
                                q_spill(hidx)

                        for i in range(ST_CUT - 8):
                            emit_stat(i)
                        for f_m in range(4):
                            v_cast(f_m)
                        for i in range(ST_CUT - 8, ST_CUT):
                            emit_stat(i)
                        for f_m in range(4, 8):
                            v_cast(f_m)
                        k_st = ST_CUT
                        for f_m in range(8, nE):
                            v_cast(f_m)
                            for _ in range(3):
                                if k_st < N_QT:
                                    emit_stat(k_st)
                                    k_st += 1
                        while k_st < N_QT:
                            emit_stat(k_st)
                            k_st += 1
                        s.wait_ge(v_vv.h, nE)
                        c_v.issue(s, hap(v_h, (nE - 1) * P * S, [[S, P], [1, S]]),
                                  spillb[:, ((nE - 1) % 2) * 2048:
                                         ((nE - 1) % 2) * 2048 + 2048])
                        s.wait_ge(t_fin.h, 1)
                        s.copy(tvals[:], psA[0:1, 0:4]).then_inc(a_tcp.h, 1)
                        a_tcp.inc(1)

                with nc.sbuf_tensor("qout", [P, 4 * 1024], BF16) as qout, \
                     nc.sbuf_tensor("qin2a", [P, 2 * 1024], F32) as qin2:
                    if upto >= 2:
                        with nc.psum_tensor("psS0", [P, 2048], F32) as psS0, \
                             nc.psum_tensor("psS1", [P, 2048], F32) as psS1, \
                             nc.sbuf_tensor("qsl", [P, 2 * 2048], BF16) as qsl, \
                             nc.Block("p2", no_gpsimd_drain=True) as blk:

                            @blk.gpsimd
                            def _(g):
                                g.wait_ge(a_wv.h, nE)
                                g.wait_ge(g_bc.h, 1)
                                c_q.wait_all(g)
                                c_v.wait_all(g)
                                emit_quant_loads(g, 0, QT_P2)

                            @blk.sync
                            def _(sy):
                                c_q.wait_all(sy)
                                c_v.wait_first_n(sy, nE)
                                for e in range(nE):
                                    if e >= 2:
                                        sy.wait_ge(t_s.h, e - 1)
                                    s_qsl.issue(
                                        sy, qsl[:, (e % 2) * 2048:(e % 2) * 2048 + 2048],
                                        hap(q_h, e * P, [[E, P], [P * E, nS], [1, P]]))
                                    emit_quant_spills(sy, max(0, ((e - 1) * QT_P2) // nE))
                                emit_quant_spills(sy, QT_P2)
                                c_v.wait_all(sy)
                                sy.wait_ge(t_s.h, 16)
                                for j in (0, 2):
                                    s_vre.issue_idx(
                                        sy, big1[:, j * 4 * S:(j + 1) * 4 * S],
                                        hap(v_h, j * 4 * P * S,
                                            [[S, P], [P * S, 4], [1, S]]), j)

                            @blk.tensor
                            def _(t):
                                t.wait_ge(v_wacc.h, 2)
                                t.matmul(psS1[0:P, 0:8], ones_f2[0:1, 0:P],
                                         tq[0:1, 0:8], start=True, stop=True
                                         ).then_inc(g_bc.h, 1)
                                g_bc.inc(1)
                                for e in range(nE):
                                    s_qsl.wait_item(t, e)
                                    if e >= 2:
                                        t.wait_ge(a_exp.h, e - 1)
                                    if e == 1:
                                        t.wait_ge(v_wacc.h, 3)
                                    ps = psS0 if e % 2 == 0 else psS1
                                    for s_k in range(nS):
                                        for nb in range(4):
                                            mm = t.matmul(
                                                ps[:, nb * 512:nb * 512 + 512],
                                                qsl[:, (e % 2) * 2048 + s_k * P:
                                                    (e % 2) * 2048 + s_k * P + P],
                                                big1[:, s_k * S + nb * 512:
                                                     s_k * S + nb * 512 + 512],
                                                start=(s_k == 0), stop=(s_k == nS - 1))
                                            if s_k == nS - 1 and nb == 3:
                                                mm.then_inc(t_s.h, 1); t_s.inc(1)

                            @blk.vector
                            def _(v):
                                c_q.wait_all(v)
                                c_v.wait_all(v)
                                v.wait_ge(a_tcp.h, 1)

                                def ts1(dst, a, s, op):
                                    v.tensor_scalar(dst, a, s, None, op0=op)
                                    v.drain()

                                def tt1(dst, a, b, op):
                                    v.tensor_tensor(dst, a, b, op=op)
                                    v.drain()

                                W = lambda i: thw[:, i:i + 1]
                                for j in range(2):
                                    mh, ml, ga, de, mh1, mh2, T1, r, p, tmp =                                     W(0), W(1), W(2), W(3), W(4), W(5), W(6), W(7), W(8), W(9)
                                    ts1(mh, tvals[:, 2 * j:2 * j + 1], INV24, ALU.mult)
                                    ts1(ml, tvals[:, 2 * j + 1:2 * j + 2], INV24, ALU.mult)
                                    ts1(ga, mh, 4097.0, ALU.mult)
                                    tt1(de, ga, mh, ALU.subtract)
                                    tt1(mh1, ga, de, ALU.subtract)
                                    tt1(mh2, mh, mh1, ALU.subtract)
                                    ts1(T1, mh, C_HI, ALU.mult)
                                    ts1(r, mh1, CH1, ALU.mult)
                                    tt1(r, r, T1, ALU.subtract)
                                    ts1(p, mh2, CH1, ALU.mult)
                                    tt1(r, r, p, ALU.add)
                                    ts1(p, mh1, CH2, ALU.mult)
                                    tt1(r, r, p, ALU.add)
                                    ts1(p, mh2, CH2, ALU.mult)
                                    tt1(r, r, p, ALU.add)
                                    ts1(p, ml, C_HI, ALU.mult)
                                    tt1(r, r, p, ALU.add)
                                    ts1(p, mh, C_LO, ALU.mult)
                                    tt1(r, r, p, ALU.add)
                                    th, tl = W(10), W(11)
                                    tt1(th, T1, r, ALU.add)
                                    tt1(p, T1, th, ALU.subtract)
                                    tt1(tl, p, r, ALU.add)
                                    v.tensor_copy(tq[:, 4 * j:4 * j + 1], th)
                                    v.tensor_copy(tq[:, 4 * j + 1:4 * j + 2], tl)
                                    v.drain()
                                    ts1(tq[:, 4 * j + 2:4 * j + 3], th, -1.0, ALU.mult)
                                    ts1(tq[:, 4 * j + 3:4 * j + 4], tl, -1.0, ALU.mult)
                                v.nop()
                                v.tensor_copy(thw[:, 0:1], tq[:, 0:1]).then_inc(v_wacc.h, 1)
                                v_wacc.inc(1)
                                v.wait_ge(g_bc.h, 1)
                                v.tensor_copy(tbc[:], psS1[0:P, 0:8]).then_inc(v_wacc.h, 1)
                                v_wacc.inc(1)
                                v.drain()
                                for e in range(nE):
                                    v.wait_ge(t_s.h, e + 1)
                                    if e >= 2:
                                        v.wait_ge(a_exp.h, e - 1)
                                    ps = psS0 if e % 2 == 0 else psS1
                                    v.reduce_max(nmax8[:, e % 2:e % 2 + 1], ps[:], axis=AX)
                                    v.drain()
                                    v.tensor_scalar(nmax8[:, e % 2:e % 2 + 1],
                                                    nmax8[:, e % 2:e % 2 + 1],
                                                    -0.125, None, op0=ALU.mult
                                                    ).then_inc(v_rm.h, 1)
                                    v_rm.inc(1)
                                    if e >= 1:
                                        v.wait_ge(a_exp.h, e)
                                        v.reciprocal(recip_all[:, e - 1:e],
                                                     rsum[:, (e - 1) % 2:(e - 1) % 2 + 1]
                                                     ).then_inc(v_rc.h, 1)
                                        v_rc.inc(1)
                                    emit_quant_vec(v, (e * QT_P2) // nE,
                                                   ((e + 1) * QT_P2) // nE)
                                emit_quant_flush(v)
                                v.wait_ge(a_exp.h, nE)
                                v.reciprocal(recip_all[:, nE - 1:nE],
                                             rsum[:, (nE - 1) % 2:(nE - 1) % 2 + 1]
                                             ).then_inc(v_rc.h, 1)
                                v_rc.inc(1)

                            @blk.scalar
                            def _(s):
                                for e in range(nE):
                                    s.wait_ge(v_rm.h, e + 1)
                                    if e >= 2:
                                        s.wait_ge(v_rc.h, e - 1)
                                    ps = psS0 if e % 2 == 0 else psS1
                                    s.activation(big0[:, e * S:e * S + S], ps[:], AF.Exp,
                                                 bias=nmax8[:, e % 2:e % 2 + 1],
                                                 scale=0.125,
                                                 accum_out=rsum[:, e % 2:e % 2 + 1]
                                                 ).then_inc(a_exp.h, 1)
                                    a_exp.inc(1)
                                    if e == nE - 1:
                                        c_v.wait_all(s)
                                        s.wait_ge(t_s.h, 16)
                                        for j in (1, 3):
                                            s_vre.issue_idx(
                                                s, big1[:, j * 4 * S:(j + 1) * 4 * S],
                                                hap(v_h, j * 4 * P * S,
                                                    [[S, P], [P * S, 4], [1, S]]), j)

                    if upto >= 3:
                        with nc.psum_tensor("psO", [P, 2048], F32) as psO, \
                             nc.psum_tensor("psTA", [P, 2048], BF16) as psTA, \
                             nc.psum_tensor("psTB", [P, 2048], BF16) as psTB, \
                             nc.sbuf_tensor("atsl", [P, 2 * 2048], BF16) as atsl, \
                             nc.Block("p3", no_gpsimd_drain=True) as blk:

                            @blk.gpsimd
                            def _(g):
                                emit_quant_loads(g, QT_P2, QT_P2 + QT_P3)

                            @blk.sync
                            def _(sy):
                                for e_m in range(nE):
                                    emit_quant_spills(sy, QT_P2 + (e_m * QT_P3) // nE)
                                emit_quant_spills(sy, QT_P2 + QT_P3)

                            @blk.tensor
                            def _(t):
                                g_id.wait_all(t)

                                def trsp(e_m):
                                    pt = psTA if e_m % 2 == 0 else psTB
                                    if e_m >= 2:
                                        t.wait_ge(a_trs.h, e_m - 1)
                                    for f_k in range(nE):
                                        mm = t.transpose(pt[:, f_k * P:f_k * P + P],
                                                         big0[:, e_m * S + f_k * P:
                                                              e_m * S + f_k * P + P],
                                                         ident_bf[:])
                                        if f_k == nE - 1:
                                            mm.then_inc(t_tr.h, 1); t_tr.inc(1)

                                trsp(0)
                                trsp(1)
                                trsp(2)
                                trsp(3)
                                for e_m in range(nE):
                                    asl = atsl[:, (e_m % 2) * 2048:
                                               (e_m % 2) * 2048 + 2048]
                                    t.wait_ge(a_trs.h, e_m + 1)
                                    if e_m >= 1:
                                        t.wait_ge(a_oev.h, e_m)
                                    for f_k in range(nE):
                                        if e_m == 0 and f_k % 4 == 0:
                                            s_vre.wait_first_n(t, f_k // 4 + 1)
                                        for sn in range(4):
                                            mm = t.matmul(
                                                psO[:, sn * 512:sn * 512 + 512],
                                                asl[:, f_k * P:f_k * P + P],
                                                big1[:, f_k * S + sn * 512:
                                                     f_k * S + sn * 512 + 512],
                                                start=(f_k == 0), stop=(f_k == nE - 1))
                                            if f_k == nE - 1 and sn == 3:
                                                mm.then_inc(t_o.h, 1); t_o.inc(1)
                                    if e_m >= 2 and e_m + 2 < nE:
                                        trsp(e_m + 2)

                            @blk.vector
                            def _(v):
                                for e_m in range(nE):
                                    emit_quant_vec(v, QT_P2 + (e_m * QT_P3) // nE,
                                                   QT_P2 + ((e_m + 1) * QT_P3) // nE)
                                emit_quant_flush(v)

                            @blk.scalar
                            def _(s):
                                s.wait_ge(v_rc.h, 16)
                                for e_m in range(nE):
                                    if e_m == 0:
                                        s.wait_ge(t_tr.h, 1)
                                        s.copy(atsl[:, 0:2048],
                                               psTA[:]).then_inc(a_trs.h, 1)
                                        a_trs.inc(1)
                                    if e_m + 1 < nE:
                                        s.wait_ge(t_tr.h, e_m + 2)
                                        if e_m >= 1:
                                            s.wait_ge(t_o.h, e_m)
                                        pt = psTA if (e_m + 1) % 2 == 0 else psTB
                                        s.copy(atsl[:, ((e_m + 1) % 2) * 2048:
                                                     ((e_m + 1) % 2) * 2048 + 2048],
                                               pt[:]).then_inc(a_trs.h, 1)
                                        a_trs.inc(1)
                                    s.wait_ge(t_o.h, e_m + 1)
                                    s.activation(big0[:, e_m * S:e_m * S + S],
                                                 psO[:], AF.Copy,
                                                 scale=recip_all[:, e_m:e_m + 1]
                                                 ).then_inc(a_oev.h, 1)
                                    a_oev.inc(1)

            with nc.sbuf_tensor("qout2", [P, 4 * 1024], BF16) as qout, \
                 nc.sbuf_tensor("qin2b", [P, 2 * 1024], F32) as qin2:
                if upto >= 4:
                    QT4_LO = QT_P2 + QT_P3
                    with contextlib.ExitStack() as st4:
                        ec4 = st4.enter_context
                        psy0 = ec4(nc.psum_tensor("psy0", [P, 512], F32))
                        psy1 = ec4(nc.psum_tensor("psy1", [P, 512], F32))
                        psy2 = ec4(nc.psum_tensor("psy2", [P, 512], F32))
                        psy3 = ec4(nc.psum_tensor("psy3", [P, 512], F32))
                        psy4 = ec4(nc.psum_tensor("psy4", [P, 512], F32))
                        psy5 = ec4(nc.psum_tensor("psy5", [P, 512], F32))
                        psT4 = ec4(nc.psum_tensor("psT4", [P, 2048], BF16))
                        xres = ec4(nc.sbuf_tensor("xres", [P, 2 * 2048], F32))
                        wobuf2 = ec4(nc.sbuf_tensor("wobuf2", [P, 16384], BF16))
                        x2t = ec4(nc.sbuf_tensor("x2t", [P, 2 * 2048], F32))
                        x2b = ec4(nc.sbuf_tensor("x2b", [P, 2 * 2048], BF16))
                        blk = ec4(nc.Block("p4", no_gpsimd_drain=True))

                        @blk.gpsimd
                        def _(g):
                            for j in range(2):
                                g_wo2.issue(
                                    g, wobuf2[:, j * 8192:(j + 1) * 8192],
                                    hap(wo, (8 + j * 4) * P * E,
                                        [[E, P], [P * E, 4], [1, E]]))
                            for s_m in range(nS):
                                if s_m >= 2:
                                    g.wait_ge(a_ln.h, s_m - 1)
                                s_x.issue(
                                    g, xres[:, (s_m % 2) * 2048:(s_m % 2) * 2048 + 2048],
                                    hap(x_tm, s_m * P * E, [[E, P], [1, E]]))
                                emit_quant_loads(
                                    g, QT4_LO + 2 * ((s_m * (QT_W1 - QT4_LO)) // nS // 2),
                                    QT4_LO + 2 * (((s_m + 1) * (QT_W1 - QT4_LO)) // nS // 2))

                        @blk.sync
                        def _(sy):
                            for s_m in range(nS):
                                emit_quant_spills(
                                    sy, QT4_LO + (s_m * (QT_W1 - QT4_LO)) // nS)
                            emit_quant_spills(sy, QT_W1)

                        @blk.tensor
                        def _(t):
                            g_wo.wait_all(t)
                            g_id.wait_all(t)

                            def x2_transposes(pm):
                                t.wait_ge(a_x2b.h, pm + 1)
                                if pm >= 1:
                                    t.wait_ge(a_tfx.h, pm)
                                xb = x2b[:, (pm % 2) * 2048:(pm % 2) * 2048 + 2048]
                                for d_k in range(nE):
                                    mm = t.transpose(psT4[:, d_k * P:d_k * P + P],
                                                     xb[:, d_k * P:d_k * P + P],
                                                     ident_bf[:])
                                    if d_k == nE - 1:
                                        mm.then_inc(t_tr4.h, 1); t_tr4.inc(1)

                            psy = [psy0, psy1, psy2, psy3, psy4, psy5]
                            t.wait_ge(a_oev.h, nE)
                            for s_m in range(nS):
                                if s_m >= 1:
                                    t.wait_ge(v_z.h, max(0, 4 * s_m - 2))
                                for e_k in range(nE):
                                    if s_m == 0 and e_k == 8:
                                        g_wo2.wait_first_n(t, 1)
                                    if s_m == 0 and e_k == 12:
                                        g_wo2.wait_first_n(t, 2)
                                    wsrc = wbuf if e_k < 8 else wobuf2
                                    wof = (e_k % 8) * E
                                    for nb in range(4):
                                        mm = t.matmul(
                                            psy[(4 * s_m + nb) % 6][:],
                                            big0[:, e_k * S + s_m * P:e_k * S + s_m * P + P],
                                            wsrc[:, wof + nb * 512:wof + nb * 512 + 512],
                                            start=(e_k == 0), stop=(e_k == nE - 1))
                                        if e_k == nE - 1 and nb == 3:
                                            mm.then_inc(t_y.h, 1); t_y.inc(1)
                                if s_m >= 2:
                                    x2_transposes(s_m - 2)
                            x2_transposes(nS - 2)
                            x2_transposes(nS - 1)
                            t.wait_ge(a_b1.h, 1)
                            t.wait_ge(a_tfx.h, 16)
                            t.transpose(psT4[:, 0:nF], b1tmpb[:], ident_bf[0:nF, 0:nF]
                                        ).then_inc(t_fin.h, 1)
                            t_fin.inc(1)

                        @blk.vector
                        def _(v):
                            c_wq.wait_first_n(v, (QT_P2 + QT_P3) // 2)
                            psy = [psy0, psy1, psy2, psy3, psy4, psy5]
                            for s_m in range(nS):
                                slot = x2t[:, (s_m % 2) * 2048:(s_m % 2) * 2048 + 2048]
                                lo = QT4_LO + (s_m * (QT_W1 - QT4_LO)) // nS
                                hi = QT4_LO + ((s_m + 1) * (QT_W1 - QT4_LO)) // nS
                                mid = (lo + hi) // 2
                                v.wait_ge(t_y.h, s_m + 1)
                                s_x.wait_item(v, s_m)
                                if s_m >= 2:
                                    c_x2.wait_item(v, s_m - 2)
                                    v.wait_ge(a_x2b.h, s_m - 1)
                                    v.wait_ge(a_ln.h, s_m - 1)
                                for nb in range(4):
                                    v.scalar_tensor_tensor(
                                        slot[:, nb * 512:nb * 512 + 512],
                                        psy[(4 * s_m + nb) % 6][:], 0.0,
                                        xres[:, (s_m % 2) * 2048 + nb * 512:
                                             (s_m % 2) * 2048 + nb * 512 + 512],
                                        op0=ALU.add, op1=ALU.add,
                                        accum_out=stats_acc[:, 4 * (s_m % 2) + nb:
                                                            4 * (s_m % 2) + nb + 1]
                                        ).then_inc(v_z.h, 1)
                                    v_z.inc(1)
                                emit_quant_vec(v, lo, mid)
                                v.wait_ge(a_ln.h, s_m + 1)
                                v.reciprocal(rec_t[:], ve_t[:]).then_inc(v_rp.h, 1)
                                v_rp.inc(1)
                                emit_quant_vec(v, mid, hi)
                                v.wait_ge(a_sq.h, s_m + 1)
                                v.tensor_scalar(slot, slot, rstd_t[:, 0:1], mr_t[:, 0:1],
                                                op0=ALU.mult, op1=ALU.add
                                                ).then_inc(v_x2.h, 1)
                                v_x2.inc(1)
                            emit_quant_flush(v)

                        @blk.scalar
                        def _(s):
                            s_misc.wait_item(s, 0)
                            s.copy(b1tmpb[:], b1tmp[:]).then_inc(a_b1.h, 1)
                            a_b1.inc(1)

                            def x2fm_evacs(pm):
                                s.wait_ge(t_tr4.h, pm + 1)
                                s.wait_ge(t_y.h, pm + 1)
                                s.copy(bass.AP(big0, pm * P, [[nE * S, P], [S, nE], [1, P]]),
                                       psT4[:]).then_inc(a_tfx.h, 1)
                                a_tfx.inc(1)

                            for s_m in range(nS):
                                sink = xres[:, (s_m % 2) * 2048:(s_m % 2) * 2048 + 2048]
                                slot = x2t[:, (s_m % 2) * 2048:(s_m % 2) * 2048 + 2048]
                                s.wait_ge(v_z.h, 4 * (s_m + 1))
                                if s_m >= 1:
                                    s.wait_ge(v_rp.h, s_m)
                                s.activation(sink, slot, AF.Square,
                                             accum_out=mv1[:, 1:2])
                                s.drain()
                                qb = 4 * (s_m % 2)
                                s.add(st6[:, 6:7], stats_acc[:, qb:qb + 1],
                                      stats_acc[:, qb + 1:qb + 2])
                                s.drain()
                                s.add(st6[:, 7:8], stats_acc[:, qb + 2:qb + 3],
                                      stats_acc[:, qb + 3:qb + 4])
                                s.drain()
                                s.add(mv1[:, 0:1], st6[:, 6:7], st6[:, 7:8])
                                s.drain()
                                s.mul(st6[:, 0:1], mv1[:, 0:1], 1.0 / 2048.0)
                                s.mul(st6[:, 1:2], mv1[:, 1:2], 1.0 / 2048.0)
                                s.drain()
                                s.square(st6[:, 2:3], st6[:, 0:1])
                                s.mul(st6[:, 4:5], st6[:, 0:1], -1.0)
                                s.drain()
                                s.activation(st6[:, 3:4], st6[:, 2:3], AF.Identity,
                                             scale=-1.0, bias=st6[:, 5:6])
                                s.drain()
                                s.add(ve_t[:], st6[:, 1:2], st6[:, 3:4]
                                      ).then_inc(a_ln.h, 1)
                                a_ln.inc(1)
                                s.wait_ge(v_rp.h, s_m + 1)
                                if s_m >= 1:
                                    s.wait_ge(v_x2.h, s_m)
                                s.activation(rstd_t[:], rec_t[:], AF.Sqrt)
                                s.drain()
                                s.mul(mr_t[:], st6[:, 4:5], rstd_t[:, 0:1]
                                      ).then_inc(a_sq.h, 1)
                                a_sq.inc(1)
                                s.wait_ge(v_x2.h, s_m + 1)
                                if s_m >= 2:
                                    s.wait_ge(t_tr4.h, s_m - 1)
                                s.copy(x2b[:, (s_m % 2) * 2048:(s_m % 2) * 2048 + 2048],
                                       x2t[:, (s_m % 2) * 2048:(s_m % 2) * 2048 + 2048]
                                       ).then_inc(a_x2b.h, 1)
                                a_x2b.inc(1)
                                c_x2.issue(s, hap(x2_h, s_m * P * E, [[E, P], [1, E]]),
                                           x2t[:, (s_m % 2) * 2048:(s_m % 2) * 2048 + 2048])
                                if s_m >= 2:
                                    x2fm_evacs(s_m - 2)
                            x2fm_evacs(nS - 2)
                            x2fm_evacs(nS - 1)
                            s.wait_ge(t_fin.h, 2)
                            s.copy(b1sb[:, 0:nF], psT4[:, 0:nF]).then_inc(a_b1.h, 1)
                            a_b1.inc(1)

                if upto >= 5:
                    with nc.psum_tensor("psH0", [P, 2048], F32) as psH0, \
                         nc.psum_tensor("psH1", [P, 2048], F32) as psH1, \
                         nc.sbuf_tensor("w1sl", [P, 2 * 2048], BF16) as w1sl, \
                         nc.sbuf_tensor("hstg", [P, 2 * 2048], BF16) as hstg, \
                         nc.Block("f1", no_gpsimd_drain=True) as blk:

                        @blk.gpsimd
                        def _(g):
                            emit_quant_loads(g, QT_W1, N_QT)

                        @blk.sync
                        def _(sy):
                            c_x2.wait_all(sy)
                            for f_m in range(nF):
                                if f_m == 0:
                                    c_wq.wait_first_n(sy, QT_W1 // 2)
                                if f_m >= 2:
                                    sy.wait_ge(t_h.h, f_m - 1)
                                s_w1sl.issue(
                                    sy, w1sl[:, (f_m % 2) * 2048:(f_m % 2) * 2048 + 2048],
                                    hap(w1q_h, f_m * P, [[FF, P], [P * FF, nE], [1, P]]))
                                if f_m >= 1:
                                    emit_quant_spills(sy, QT_W1 + 2 * (f_m - 1))
                            emit_quant_spills(sy, N_QT)

                        @blk.vector
                        def _(v):
                            for f_m in range(nF):
                                emit_quant_vec(v, QT_W1 + 2 * f_m, QT_W1 + 2 * f_m + 2)
                            emit_quant_flush(v)

                        @blk.tensor
                        def _(t):
                            for f_m in range(nF):
                                s_w1sl.wait_item(t, f_m)
                                if f_m >= 2:
                                    t.wait_ge(a_hh.h, f_m - 1)
                                ps = psH0 if f_m % 2 == 0 else psH1
                                for d_k in range(nE):
                                    for nb in range(4):
                                        mm = t.matmul(
                                            ps[:, nb * 512:nb * 512 + 512],
                                            w1sl[:, (f_m % 2) * 2048 + d_k * P:
                                                 (f_m % 2) * 2048 + d_k * P + P],
                                            big0[:, d_k * S + nb * 512:
                                                 d_k * S + nb * 512 + 512],
                                            start=(d_k == 0), stop=(d_k == nE - 1))
                                        if d_k == nE - 1 and nb == 3:
                                            mm.then_inc(t_h.h, 1); t_h.inc(1)

                        @blk.scalar
                        def _(s):
                            for f_m in range(nF):
                                s.wait_ge(t_h.h, f_m + 1)
                                if f_m >= 2:
                                    c_h.wait_item(s, f_m - 2)
                                ps = psH0 if f_m % 2 == 0 else psH1
                                s.activation(hstg[:, (f_m % 2) * 2048:
                                                  (f_m % 2) * 2048 + 2048],
                                             ps[:], AF.Gelu, bias=b1sb[:, f_m:f_m + 1]
                                             ).then_inc(a_hh.h, 1)
                                a_hh.inc(1)
                                s.wait_ge(a_hh.h, f_m + 1)
                                c_h.issue(s, hap(h_h, f_m * P * S, [[S, P], [1, S]]),
                                          hstg[:, (f_m % 2) * 2048:(f_m % 2) * 2048 + 2048])

        if upto >= 6:
            FG = 8
            nhalf = nS // 2
            with nc.psum_tensor("psF0", [P, 2048], F32) as psF0, \
                 nc.psum_tensor("psF1", [P, 2048], F32) as psF1, \
                 nc.sbuf_tensor("oacc", [P, 8 * 2048], F32) as oacc, \
                 nc.sbuf_tensor("w2g", [P, 2 * 8 * 2048], BF16) as w2g, \
                 nc.sbuf_tensor("hg", [P, 2 * 8 * 1024], BF16) as hg, \
                 nc.sbuf_tensor("lnsink", [P, 2048], F32) as lnsink, \
                 nc.sbuf_tensor("outt", [P, 2 * 2048], F32) as outt, \
                 nc.sbuf_tensor("b2bf", [1, E], BF16) as b2bf, \
                 nc.Block("f2", no_gpsimd_drain=True) as blk:

                @blk.gpsimd
                def _(g):
                    g_id.issue(g, b2bf[:], hap(b2, 0, [[E, 1], [1, E]]))

                @blk.sync
                def _(sy):
                    c_wq.wait_all(sy)
                    c_h.wait_all(sy)

                    def grp_loads(gi):
                        half, g = gi // FG, gi % FG
                        if gi >= 2:
                            sy.wait_ge(t_f.h, (gi - 1) * nhalf)
                        c_h.wait_first_n(sy, 8 * (g + 1))
                        slot = gi % 2
                        if gi == 0:
                            s_w2g.issue_idx(
                                sy, w2g[:, 0:8192],
                                hap(w2q_h, 0, [[E, P], [P * E, 4], [1, E]]), 0)
                            s_misc.issue_idx(
                                sy, hg[:, 4096:8192],
                                hap(h_h, 4 * P * S,
                                    [[S, P], [P * S, 4], [1, 1024]]), 2)
                        else:
                            s_w2g.issue_idx(
                                sy, w2g[:, slot * 16384:slot * 16384 + 16384],
                                hap(w2q_h, (g * 8) * P * E,
                                    [[E, P], [P * E, 8], [1, E]]), gi)
                        if gi > 0:
                            s_hg.issue_idx(
                                sy, hg[:, slot * 8192:slot * 8192 + 8192],
                                hap(h_h, (g * 8) * P * S + half * 1024,
                                    [[S, P], [P * S, 8], [1, 1024]]), gi)

                    grp_loads(0)
                    c_x2.wait_all(sy)
                    for j in range(nhalf):
                        s_x2i.issue(sy, oacc[:, j * 2048:j * 2048 + 2048],
                                    hap(x2_h, j * P * E, [[E, P], [1, E]]))
                    for gi in range(1, FG):
                        grp_loads(gi)
                    grp_loads(FG)
                    for j in range(nhalf):
                        sy.wait_ge(v_ot.h, j + 1)
                        s_x2i.issue(sy, oacc[:, j * 2048:j * 2048 + 2048],
                                    hap(x2_h, (nhalf + j) * P * E, [[E, P], [1, E]]))
                    for gi in range(FG + 1, 2 * FG):
                        grp_loads(gi)

                @blk.tensor
                def _(t):
                    g_id.wait_all(t)
                    for half in range(2):
                        for g in range(FG):
                            gi = half * FG + g
                            slot = gi % 2
                            s_w2g.wait_item(t, gi)
                            s_hg.wait_item(t, gi)
                            for j in range(nhalf):
                                ti = gi * nhalf + j
                                if ti >= 2:
                                    t.wait_ge(v_fa.h, ti - 1)
                                ps = psF0 if ti % 2 == 0 else psF1
                                if g == 0:
                                    for nb in range(4):
                                        t.matmul(ps[:, nb * 512:nb * 512 + 512],
                                                 ones_bf[:, 0:P],
                                                 b2bf[:, nb * 512:nb * 512 + 512],
                                                 start=True, stop=False)
                                for f_k in range(8):
                                    if gi == 0 and j == 0 and f_k == 4:
                                        s_misc.wait_item(t, 1)
                                        s_misc.wait_item(t, 2)
                                    for nb in range(4):
                                        mm = t.matmul(
                                            ps[:, nb * 512:nb * 512 + 512],
                                            hg[:, slot * 8192 + f_k * 1024 + j * P:
                                                slot * 8192 + f_k * 1024 + j * P + P],
                                            w2g[:, slot * 16384 + f_k * E + nb * 512:
                                                slot * 16384 + f_k * E + nb * 512 + 512],
                                            start=(g != 0 and f_k == 0),
                                            stop=(f_k == 7))
                                        if f_k == 7 and nb == 3:
                                            mm.then_inc(t_f.h, 1); t_f.inc(1)

                @blk.vector
                def _(v):
                    def ln2_recip(hj):
                        v.wait_ge(a_ln2.h, hj + 1)
                        if hj >= 1:
                            v.wait_ge(a_sq2.h, hj)
                        v.reciprocal(rec_t[:], ve_t[:]).then_inc(v_rp2.h, 1)
                        v_rp2.inc(1)

                    def ln2_affine(hj):
                        j = hj % nhalf
                        v.wait_ge(a_sq2.h, hj + 1)
                        if hj >= 2:
                            c_y.wait_item(v, hj - 2)
                        v.tensor_scalar(outt[:, (hj % 2) * 2048:
                                            (hj % 2) * 2048 + 2048],
                                        oacc[:, j * 2048:j * 2048 + 2048],
                                        rstd_t[:, 0:1], mr_t[:, 0:1],
                                        op0=ALU.mult, op1=ALU.add
                                        ).then_inc(v_ot.h, 1)
                        v_ot.inc(1)

                    for half in range(2):
                        for g in range(FG):
                            gi = half * FG + g
                            for j in range(nhalf):
                                ti = gi * nhalf + j
                                v.wait_ge(t_f.h, ti + 1)
                                if g == 0:
                                    s_x2i.wait_item(v, half * nhalf + j)
                                ps = psF0 if ti % 2 == 0 else psF1
                                if g == FG - 1:
                                    v.scalar_tensor_tensor(
                                        oacc[:, j * 2048:j * 2048 + 2048], ps[:],
                                        0.0, oacc[:, j * 2048:j * 2048 + 2048],
                                        op0=ALU.add, op1=ALU.add,
                                        accum_out=stats_acc[:, 8 + j:9 + j]
                                        ).then_inc(v_fa.h, 1)
                                else:
                                    v.tensor_tensor(
                                        oacc[:, j * 2048:j * 2048 + 2048], ps[:],
                                        oacc[:, j * 2048:j * 2048 + 2048],
                                        op=ALU.add).then_inc(v_fa.h, 1)
                                v_fa.inc(1)
                                if g == FG - 1:
                                    if j >= 4:
                                        ln2_affine(half * nhalf + j - 4)
                                    if j >= 2:
                                        ln2_recip(half * nhalf + j - 2)
                        for j in range(nhalf, nhalf + 4):
                            ln2_affine(half * nhalf + j - 4)
                            if j - 2 < nhalf:
                                ln2_recip(half * nhalf + j - 2)

                @blk.scalar
                def _(s):
                    c_h.wait_all(s)
                    c_wq.wait_all(s)
                    s_hg.issue_idx(
                        s, hg[:, 0:4096],
                        hap(h_h, 0, [[S, P], [P * S, 4], [1, 1024]]), 0)
                    s_misc.issue_idx(
                        s, w2g[:, 8192:16384],
                        hap(w2q_h, 4 * P * E, [[E, P], [P * E, 4], [1, E]]), 1)
                    nm_col = [st6[:, 4 + 2 * (k % 2):5 + 2 * (k % 2)] for k in range(2)]

                    def ln2_stats(hj):
                        half, j = hj // nhalf, hj % nhalf
                        strip = oacc[:, j * 2048:j * 2048 + 2048]
                        last_ti = ((half + 1) * FG - 1) * nhalf + j
                        s.wait_ge(v_fa.h, last_ti + 1)
                        if hj >= 1:
                            s.wait_ge(v_rp2.h, hj)
                        s.activation(lnsink[:], strip, AF.Square,
                                     accum_out=mv1[:, 1:2])
                        s.drain()
                        s.mul(st6[:, 0:1], stats_acc[:, 8 + j:9 + j], 1.0 / 2048.0)
                        s.mul(st6[:, 1:2], mv1[:, 1:2], 1.0 / 2048.0)
                        s.drain()
                        s.square(st6[:, 2:3], st6[:, 0:1])
                        s.mul(nm_col[hj % 2][:], st6[:, 0:1], -1.0)
                        s.drain()
                        s.activation(st6[:, 3:4], st6[:, 2:3], AF.Identity,
                                     scale=-1.0, bias=st6[:, 5:6])
                        s.drain()
                        s.add(ve_t[:], st6[:, 1:2], st6[:, 3:4]
                              ).then_inc(a_ln2.h, 1)
                        a_ln2.inc(1)

                    def ln2_sqrt(hj):
                        s.wait_ge(v_rp2.h, hj + 1)
                        if hj >= 1:
                            s.wait_ge(v_ot.h, hj)
                        s.activation(rstd_t[:], rec_t[:], AF.Sqrt)
                        s.drain()
                        s.mul(mr_t[:], nm_col[hj % 2][:], rstd_t[:, 0:1]
                              ).then_inc(a_sq2.h, 1)
                        a_sq2.inc(1)

                    def ln2_spill(hj):
                        s.wait_ge(v_ot.h, hj + 1)
                        c_y.issue(s, hap(y, hj * P * E, [[E, P], [1, E]]),
                                  outt[:, (hj % 2) * 2048:(hj % 2) * 2048 + 2048])

                    for half in range(2):
                        for k in range(nhalf):
                            hj = half * nhalf + k
                            ln2_stats(hj)
                            if k >= 1:
                                ln2_sqrt(hj - 1)
                            if k >= 2:
                                ln2_spill(hj - 2)
                        ln2_sqrt(half * nhalf + nhalf - 1)
                        ln2_spill(half * nhalf + nhalf - 2)
                        ln2_spill(half * nhalf + nhalf - 1)
                    c_y.wait_all(s)

        with nc.Block("fin", no_gpsimd_drain=True) as blk:
            @blk.sync
            def _(sy):
                for st_ in (g_xfm, g_id, g_wq, g_wk, g_wo, g_wo2, s_wv,
                            s_stats, s_qsl, s_vre, s_osl, s_x, s_x2i, s_w1sl,
                            s_w2g, s_hg, s_qin, s_misc,
                            c_q, c_v, c_o, c_x2, c_h, c_wq, c_y):
                    st_.wait_all(sy)

    return nc


_NC_CACHE = {}


def _get_nc(upto=6, debug=False):
    key = (upto, debug)
    if key not in _NC_CACHE:
        _NC_CACHE[key] = build_nc(upto, debug)
    return _NC_CACHE[key]


def make_in_maps(x, Wq, Wk, Wv, Wo, w1T, b1v, w2T, b2v):
    ident = np.eye(P, dtype=np.float32)
    in_maps = []
    for c in range(x.shape[0]):
        xc = np.ascontiguousarray(x[c])
        in_maps.append({
            "x_tm": xc,
            "x_fm": np.ascontiguousarray(xc.T),
            "wq": Wq, "wk": Wk, "wv": Wv, "wo": Wo,
            "w1t": w1T, "b1": b1v, "w2t": w2T, "b2": b2v,
            "ident_in": ident,
        })
    return in_maps


def kernel(L0_attn_x, L0_attn_Wq, L0_attn_Wk, L0_attn_Wv, L0_attn_Wo,
           L0_ffn_w1, L0_ffn_b1, L0_ffn_w2, L0_ffn_b2):
    x = np.ascontiguousarray(np.asarray(L0_attn_x, dtype=np.float32))
    Wq = np.ascontiguousarray(np.asarray(L0_attn_Wq, dtype=np.float32))
    Wk = np.ascontiguousarray(np.asarray(L0_attn_Wk, dtype=np.float32))
    Wv = np.ascontiguousarray(np.asarray(L0_attn_Wv, dtype=np.float32))
    Wo = np.ascontiguousarray(np.asarray(L0_attn_Wo, dtype=np.float32))
    w1T = np.ascontiguousarray(np.asarray(L0_ffn_w1, dtype=np.float32).T)
    w2T = np.ascontiguousarray(np.asarray(L0_ffn_w2, dtype=np.float32).T)
    b1v = np.ascontiguousarray(np.asarray(L0_ffn_b1, dtype=np.float32))
    b2v = np.ascontiguousarray(np.asarray(L0_ffn_b2, dtype=np.float32))

    nc = _get_nc()
    in_maps = make_in_maps(x, Wq, Wk, Wv, Wo, w1T, b1v, w2T, b2v)
    res = run_bass_kernel_spmd(nc, in_maps, core_ids=list(range(8)))
    return np.stack([np.asarray(r["y"]) for r in res.results], axis=0)
